# version 14
# baseline (speedup 1.0000x reference)
import contextlib
import numpy as np

import concourse.bass as bass
import concourse.mybir as mybir
from concourse.bass_utils import run_bass_kernel_spmd

F32 = mybir.dt.float32
AF = mybir.ActivationFunctionType
ALU = mybir.AluOpType
AX = mybir.AxisListType

C = 1024
CC = 8
H = 16
HD = 8
TSEQ = 8192
TOWN = 2048
NT = 512
NTILE = TOWN // NT
NCH = TOWN // 128
GSZ = 256
NGRP = TSEQ // GSZ
ER = 15
FR = 2048
FS = 4096
NFC = 16
NFC_S = FS // 128
EPS_ATTN = 1e-6
EPS_LN = 1e-5
NRING1 = 2
NRING2 = 2
NEXP = 2
NJ = NT // 128
CAP = 512
NCAP = CAP // 128
ROWW = C + 16
BIGF = 4096.0
SUBS = [(0, 512)]
CAPE = SUBS[-1][1]
DBG_E2 = False


class Sems:
    def __init__(self, nc, ctx, names):
        self.h = {n: ctx.enter_context(nc.semaphore(n)) for n in names}
        self.v = {n: 0 for n in names}

    def inc(self, inst, name, k=1):
        inst.then_inc(self.h[name], k)
        self.v[name] += k
        return self.v[name]

    def val(self, name):
        return self.v[name]

    def wait(self, eng, name, val):
        if val > 0:
            eng.wait_ge(self.h[name], val)


def build_nc():
    nc = bass.Bass("TRN2", target_bir_lowering=False)

    xT = nc.dram_tensor("xT", [C, TOWN], F32, kind="ExternalInput").ap()
    wqkvT = nc.dram_tensor("wqkvT", [C, 3 * C], F32, kind="ExternalInput").ap()
    wprojT_d = nc.dram_tensor("wprojT", [C, C], F32, kind="ExternalInput").ap()
    bproj_d = nc.dram_tensor("bproj", [1, C], F32, kind="ExternalInput").ap()
    gw_d = nc.dram_tensor("gw", [C, ER], F32, kind="ExternalInput").ap()
    gb_d = nc.dram_tensor("gb", [1, ER], F32, kind="ExternalInput").ap()
    sw1_d = nc.dram_tensor("sw1", [NFC_S, 128, CC, 128], F32, kind="ExternalInput").ap()
    sw2_d = nc.dram_tensor("sw2", [NFC_S, 128, C], F32, kind="ExternalInput").ap()
    rw1_d = nc.dram_tensor("rw1", [ER, NFC, 128, CC, 128], F32, kind="ExternalInput").ap()
    rw2_d = nc.dram_tensor("rw2", [ER, NFC, 128, C], F32, kind="ExternalInput").ap()
    b1r_d = nc.dram_tensor("b1r", [128, ER, NFC], F32, kind="ExternalInput").ap()
    b1s_d = nc.dram_tensor("b1s", [128, NFC_S], F32, kind="ExternalInput").ap()
    b2cat_d = nc.dram_tensor("b2cat", [16, C], F32, kind="ExternalInput").ap()
    ident_d = nc.dram_tensor("ident", [128, 128], F32, kind="ExternalInput").ap()
    bc16_d = nc.dram_tensor("bc16", [16, HD, 128], F32, kind="ExternalInput").ap()
    ltri_d = nc.dram_tensor("ltri", [128, 128], F32, kind="ExternalInput").ap()
    tokid_d = nc.dram_tensor("tokid", [128, 16], mybir.dt.uint32, kind="ExternalInput").ap()
    idinit_d = nc.dram_tensor("idinit", [1, CAP], mybir.dt.uint32, kind="ExternalInput").ap()
    out = nc.dram_tensor("out", [TOWN, C], F32, kind="ExternalOutput").ap()
    x1h = nc.dram_tensor("x1h", [TOWN, ROWW], F32, kind="Internal").ap()
    ybuf = nc.dram_tensor("ybuf", [TOWN, C], F32, kind="Internal").ap()
    idh = [nc.dram_tensor(f"idh{i}", [CAP, 1], mybir.dt.uint32, kind="Internal").ap()
           for i in range(ER)]
    kvar_i = nc.dram_tensor("kvar_i", [128, HD * 128 + HD * 16], F32,
                            kind="Internal").ap()
    kvar_o = nc.dram_tensor("kvar_o", [128, HD * 128 + HD * 16], F32,
                            kind="Internal").ap()
    dbg1 = nc.dram_tensor("dbg1", [128, NCAP, ROWW], F32, kind="Internal").ap()
    dbg2 = nc.dram_tensor("dbg2", [128, CAP], F32, kind="Internal").ap()
    dbg3 = nc.dram_tensor("dbg3", [128, C], F32, kind="Internal").ap()

    xT_r = xT.rearrange("(cc p) t -> p cc t", p=128)
    wqkvT_r = wqkvT.rearrange("(cc p) e -> p cc e", p=128)
    wprojT_r = wprojT_d.rearrange("(cc p) e -> p cc e", p=128)
    gw_r = gw_d.rearrange("(cc p) e -> p cc e", p=128)

    ctx = contextlib.ExitStack()
    with ctx:
        S = Sems(nc, ctx, ["LD", "LX", "LW1_0", "LW1_1", "LW2_0", "LW2_1",
                           "PE", "DV", "AC", "GP", "ST0", "ST1", "SH", "SC",
                           "IGI", "IGS", "IGR", "IGG0", "IGG1", "CX", "CCOL", "CXB", "LKV"])
        ck = {}

        ident = ctx.enter_context(nc.sbuf_tensor("identt", [128, 128], F32))
        ones1 = ctx.enter_context(nc.sbuf_tensor("ones1", [1, 128], F32))
        ones16 = ctx.enter_context(nc.sbuf_tensor("ones16", [16, 128], F32))
        onesc = ctx.enter_context(nc.sbuf_tensor("onesc", [128, 1], F32))
        b1r_sb = ctx.enter_context(nc.sbuf_tensor("b1r_sb", [128, ER, NFC], F32))
        b1s_sb = ctx.enter_context(nc.sbuf_tensor("b1s_sb", [128, NFC_S], F32))
        b2_sb = ctx.enter_context(nc.sbuf_tensor("b2_sb", [16, C], F32))
        gw_sb = ctx.enter_context(nc.sbuf_tensor("gw_sb", [128, CC, ER], F32))
        gb_sb = ctx.enter_context(nc.sbuf_tensor("gb_sb", [1, ER], F32))
        ltri_sb = ctx.enter_context(nc.sbuf_tensor("ltri_sb", [128, 128], F32))
        tokid_sb = ctx.enter_context(nc.sbuf_tensor("tokid_sb", [128, 16],
                                                    mybir.dt.uint32))
        slotsb = ctx.enter_context(nc.sbuf_tensor("slotsb", [128, ER, 16],
                                                  mybir.dt.uint32))
        block = ctx.enter_context(nc.Block())

        mctx = contextlib.ExitStack()
        X = mctx.enter_context(nc.sbuf_tensor("X", [128, CC, TOWN], F32))
        gated_fm = mctx.enter_context(nc.sbuf_tensor("gated_fm", [16, TOWN], F32))
        R0 = mctx.enter_context(nc.sbuf_tensor("R0", [1, NT], F32))
        R1 = mctx.enter_context(nc.sbuf_tensor("R1", [1, NT], F32))
        R2 = mctx.enter_context(nc.sbuf_tensor("R2", [1, NT], F32))
        R3 = mctx.enter_context(nc.sbuf_tensor("R3", [1, NT], F32))
        R4 = mctx.enter_context(nc.sbuf_tensor("R4", [1, NT], F32))

        def _init_loads(sync):
            for dst, src in [
                (X[:, :, 0:GSZ], xT_r[:, :, 0:GSZ]),
                (ident[:], ident_d),
                (b1r_sb[:], b1r_d),
                (b1s_sb[:], b1s_d),
                (b2_sb[:], b2cat_d),
                (gw_sb[:], gw_r),
                (gb_sb[:], gb_d),
                (ltri_sb[:], ltri_d),
                (tokid_sb[:], tokid_d),
            ]:
                S.inc(sync.dma_start(out=dst, in_=src), "LD", 16)
        block.sync(_init_loads)
        LD_ALL = S.val("LD")

        def _init_memset(gp):
            S.inc(gp.memset(ones1[:], 1.0), "GP", 1)
            S.inc(gp.memset(ones16[:], 1.0), "GP", 1)
            S.inc(gp.memset(onesc[:], 1.0), "GP", 1)
        block.gpsimd(_init_memset)
        GP_INIT = S.val("GP")

        with contextlib.ExitStack() as abctx:
            kvd = abctx.enter_context(nc.sbuf_tensor("kvd", [128, HD, 128], F32))
            KS = abctx.enter_context(nc.sbuf_tensor("KS", [128, HD, 16], F32))
            bc16 = abctx.enter_context(nc.sbuf_tensor("bc16s", [16, HD, 128], F32))
            bproj_sb = abctx.enter_context(nc.sbuf_tensor("bproj_sb", [1, C], F32))
            onesr = abctx.enter_context(nc.sbuf_tensor("onesr", [1, NT], F32))

            def _ab_loads(sync):
                S.inc(sync.dma_start(out=bc16[:], in_=bc16_d), "LD", 16)
                S.inc(sync.dma_start(out=bproj_sb[:], in_=bproj_d), "LD", 16)
            block.sync(_ab_loads)

            def _ab_memset(gp):
                S.inc(gp.memset(onesr[:], 1.0), "GP", 1)
                S.inc(gp.memset(kvd[:], 0.0), "GP", 1)
                S.inc(gp.memset(KS[:], 0.0), "GP", 1)
            block.gpsimd(_ab_memset)
            GP_AB = S.val("GP")

            with contextlib.ExitStack() as actx:
                wkv = actx.enter_context(nc.sbuf_tensor("wkv", [128, CC, 2048], F32))
                Ktm = [actx.enter_context(nc.sbuf_tensor(f"Ktm{i}", [128, 1024], F32))
                       for i in range(2)]
                Vxt = [actx.enter_context(nc.sbuf_tensor(f"Vxt{i}", [128, HD, 129], F32))
                       for i in range(2)]
                sA = actx.enter_context(nc.sbuf_tensor("sA", [128, 1024], F32))
                sB = actx.enter_context(nc.sbuf_tensor("sB", [128, 1024], F32))

                pk = actx.enter_context(nc.psum_tensor("pk", [128, 1024], F32))
                pv = actx.enter_context(nc.psum_tensor("pv", [128, 1024], F32))
                pkv = [actx.enter_context(
                    nc.psum_tensor(f"pkv{i}", [128, 387 if i < 2 else 258], F32))
                    for i in range(3)]

                def kvreg(hd):
                    t_, j_ = divmod(hd, 3)
                    return pkv[t_][:, j_ * 129:(j_ + 1) * 129]

                def _vx_ones(gp):
                    for b in range(2):
                        S.inc(gp.memset(Vxt[b][:, :, 128:129], 1.0), "GP", 1)
                block.gpsimd(_vx_ones)
                GP_VX = S.val("GP")

                def _load_wkv(sync):
                    S.inc(sync.dma_start(out=wkv[:], in_=wqkvT_r[:, :, C:3 * C]),
                          "LKV", 16)
                    S.inc(sync.dma_start(out=X[:, :, GSZ:TOWN],
                                         in_=xT_r[:, :, GSZ:TOWN]), "LX", 16)
                block.sync(_load_wkv)

                first_bank = [True, True, True]

                def _kv_mms(t, j):
                    S.wait(t, "DV", ck[("A_vcp", j)])
                    for hd in range(HD):
                        st = first_bank[hd // 3]
                        if st:
                            first_bank[hd // 3] = False
                        mi = t.matmul(kvreg(hd),
                                      Ktm[j % 2][:, hd * 128:(hd + 1) * 128],
                                      Vxt[j % 2][:, hd, :],
                                      start=st, stop=(j == NCH - 1),
                                      skip_group_check=True)
                    ck[("A_kv", j)] = S.inc(mi, "PE")

                for j in range(NCH):
                        def _pe(t, j=j):
                            if j == 0:
                                S.wait(t, "LD", S.val("LD"))
                                S.wait(t, "LKV", S.val("LKV"))
                            if j == 2:
                                S.wait(t, "LX", S.val("LX"))
                            if j > 0:
                                S.wait(t, "DV", ck[("A_stt", j - 1)])
                            for cc in range(CC):
                                for hb in range(2):
                                    mi = t.matmul(
                                        pk[:, hb * 512:(hb + 1) * 512],
                                        X[:, cc, j * 128:(j + 1) * 128],
                                        wkv[:, cc, hb * 512:(hb + 1) * 512],
                                        start=(cc == 0), stop=(cc == CC - 1))
                            ck[("A_K", j)] = S.inc(mi, "PE")
                            if j > 0:
                                S.wait(t, "DV", ck[("A_vcp", j - 1)])
                            for cc in range(CC):
                                for hb in range(2):
                                    mi = t.matmul(
                                        pv[:, hb * 512:(hb + 1) * 512],
                                        X[:, cc, j * 128:(j + 1) * 128],
                                        wkv[:, cc, 1024 + hb * 512:1024 + (hb + 1) * 512],
                                        start=(cc == 0), stop=(cc == CC - 1))
                            ck[("A_V", j)] = S.inc(mi, "PE")
                            if j > 0:
                                _kv_mms(t, j - 1)
                        block.tensor(_pe)

                        def _dv1(v, j=j):
                            S.wait(v, "PE", ck[("A_K", j)])
                            if j > 0:
                                S.wait(v, "AC", ck[("A_exp", j - 1)])
                            mi = v.tensor_scalar(sA[:], pk[:], 0.0, None, ALU.min)
                            ck[("A_min", j)] = S.inc(mi, "DV")
                        block.vector(_dv1)

                        def _ac(a, j=j):
                            S.wait(a, "DV", ck[("A_min", j)])
                            if j > 0:
                                S.wait(a, "DV", ck[("A_stt", j - 1)])
                            mi = a.activation(sB[:], sA[:], AF.Exp)
                            ck[("A_exp", j)] = S.inc(mi, "AC")
                        block.scalar(_ac)

                        def _dv2(v, j=j):
                            S.wait(v, "AC", ck[("A_exp", j)])
                            if j > 1:
                                S.wait(v, "PE", ck[("A_kv", j - 2)])
                            mi = v.scalar_tensor_tensor(Ktm[j % 2][:], pk[:], 0.0,
                                                        sB[:], ALU.max, ALU.add)
                            ck[("A_stt", j)] = S.inc(mi, "DV")
                            S.wait(v, "PE", ck[("A_V", j)])
                            if j == 0:
                                S.wait(v, "GP", GP_VX)
                            mi = v.tensor_copy(
                                Vxt[j % 2][:, :, 0:128],
                                pv[:].rearrange("p (hd m) -> p hd m", m=128))
                            ck[("A_vcp", j)] = S.inc(mi, "DV")
                        block.vector(_dv2)

                def _pe_last(t):
                    _kv_mms(t, NCH - 1)
                block.tensor(_pe_last)

                def _dv_fin(v):
                    S.wait(v, "PE", ck[("A_kv", NCH - 1)])
                    S.wait(v, "GP", GP_AB)
                    for hd in range(HD):
                        reg = kvreg(hd)
                        v.tensor_copy(kvd[0:64, hd, 0:64], reg[0:64, 0:64])
                        v.tensor_copy(kvd[64:128, hd, 64:128], reg[64:128, 64:128])
                        v.tensor_copy(KS[0:64, hd, 2 * hd:2 * hd + 1],
                                      reg[0:64, 128:129])
                        mi = v.tensor_copy(KS[64:128, hd, 2 * hd + 1:2 * hd + 2],
                                           reg[64:128, 128:129])
                    ck["A_fin"] = S.inc(mi, "DV")
                block.vector(_dv_fin)

            with contextlib.ExitStack() as bctx:
                wq = bctx.enter_context(nc.sbuf_tensor("wq", [128, CC, C], F32))
                wp = bctx.enter_context(nc.sbuf_tensor("wp", [128, CC, C], F32))
                Qf = bctx.enter_context(nc.sbuf_tensor("Qf", [128, CC, NT], F32))
                Af = bctx.enter_context(nc.sbuf_tensor("Af", [128, CC, NT], F32))
                sA2 = Af[:, 6, :]
                sB2 = Af[:, 7, :]
                nrm = bctx.enter_context(nc.sbuf_tensor("nrm", [16, NT], F32))
                inv = bctx.enter_context(nc.sbuf_tensor("inv", [16, NT], F32))
                pq = [bctx.enter_context(nc.psum_tensor(f"pqb{i}", [128, NT], F32))
                      for i in range(2)]
                pnorm = bctx.enter_context(nc.psum_tensor("pnb", [16, NT], F32))

                def _load_b(sync):
                    S.wait(sync, "PE", ck[("A_kv", NCH - 1)])
                    S.inc(sync.dma_start(out=wq[:, :, 0:512],
                                         in_=wqkvT_r[:, :, 0:512]), "LKV", 16)
                    S.inc(sync.dma_start(out=wq[:, :, 512:C],
                                         in_=wqkvT_r[:, :, 512:C]), "LX", 16)
                    S.inc(sync.dma_start(out=wp[:], in_=wprojT_r), "LD", 16)
                block.sync(_load_b)
                LD_B = S.val("LD")
                LKV_B = S.val("LKV")
                LX_B = S.val("LX")

                def _kv_out(sync):
                    S.wait(sync, "DV", ck["A_fin"])
                    S.inc(sync.dma_start(out=kvar_i[:, 0:1024],
                                         in_=kvd[:].rearrange("p h m -> p (h m)")),
                          "CX", 16)
                    S.inc(sync.dma_start(out=kvar_i[:, 1024:1152],
                                         in_=KS[:].rearrange("p h m -> p (h m)")),
                          "CX", 16)
                block.sync(_kv_out)

                def _kv_cc(gp):
                    S.wait(gp, "CX", 32)
                    mi = gp.collective_compute(
                        "AllReduce", ALU.add,
                        replica_groups=[[0, 1, 2, 3], [4, 5, 6, 7]],
                        ins=[kvar_i.opt()], outs=[kvar_o.opt()])
                    ck["CC_kv"] = S.inc(mi, "CCOL")
                    S.wait(gp, "CCOL", ck["CC_kv"])
                    S.inc(gp.dma_start(out=kvd[:].rearrange("p h m -> p (h m)"),
                                       in_=kvar_o[:, 0:1024]), "CXB", 16)
                    S.inc(gp.dma_start(out=KS[:].rearrange("p h m -> p (h m)"),
                                       in_=kvar_o[:, 1024:1152]), "CXB", 16)
                block.gpsimd(_kv_cc)


                def _lnbx(tile, mid=None):
                    tsl = slice(tile * NT, (tile + 1) * NT)
                    _emit_ln(nc, block, S, ck, ("B", tile), mid=mid,
                             y_ap=lambda oc, tsl=tsl: X[:, oc, tsl],
                             sq_ap=lambda oc: Af[:, oc, :],
                             dst_ap=lambda oc, tsl=tsl: X[:, oc, tsl],
                             dep_dv=lambda oc, tile=tile: ck[("B_y1", tile, oc)],
                             dep_pe_sq=ck[("B_p", tile, CC - 1)],
                             onesc=onesc, ones1=ones1,
                             R0=R0, R1=R1, R2=R2, R3=R3, R4=R4, sq_rotate=False)

                    otmB = Af[:].rearrange("p a b -> p (a b)")
                    with contextlib.ExitStack() as pbx:
                        ptxB = [pbx.enter_context(
                            nc.psum_tensor(f"ptxB{tile}_{i}", [128, 128], F32))
                            for i in range(2)]
                        for j in range(NJ):
                            for oc in range(CC):
                                k = j * CC + oc

                                def _petxB(t, tile=tile, j=j, oc=oc, k=k):
                                    if k == 0:
                                        S.wait(t, "DV", ck[("LN", "B", tile, "app2")])
                                    if k >= 2:
                                        S.wait(t, "DV", ck[("BX_ev", tile, k - 2)])
                                    mi = t.transpose(
                                        ptxB[k % 2][:],
                                        X[:, oc,
                                          tile * NT + j * 128:tile * NT + (j + 1) * 128],
                                        ident[:])
                                    ck[("BX_tr", tile, k)] = S.inc(mi, "PE")
                                block.tensor(_petxB)

                                def _dvtxB(v, tile=tile, j=j, oc=oc, k=k):
                                    S.wait(v, "PE", ck[("BX_tr", tile, k)])
                                    prev = ck.get(("BX_st", tile, j - 1)) if j >= 1 else (
                                        ck.get(("BX_st", tile - 1, NJ - 1))
                                        if tile > 0 else 0)
                                    if prev:
                                        S.wait(v, "SH", prev)
                                    mi = v.tensor_copy(
                                        otmB[:, oc * 128:(oc + 1) * 128],
                                        ptxB[k % 2][:])
                                    ck[("BX_ev", tile, k)] = S.inc(mi, "DV")
                                block.vector(_dvtxB)

                            def _stB(sync, tile=tile, j=j):
                                S.wait(sync, "DV", ck[("BX_ev", tile, j * CC + CC - 1)])
                                r0 = tile * NT + j * 128
                                mi = sync.dma_start(out=x1h[r0:r0 + 128, 0:C],
                                                    in_=otmB[:, 0:C])
                                ck[("BX_st", tile, j)] = S.inc(mi, "SH", 16)
                            block.sync(_stB)

                for tile in range(NTILE):
                    tsl = slice(tile * NT, (tile + 1) * NT)

                    def _mid_ps1(tile=tile, tsl=tsl):
                        with contextlib.ExitStack() as ps1:
                            for ec in range(CC):
                                def _peq(t, ec=ec, tile=tile):
                                    if ec == 0 and tile == 0:
                                        S.wait(t, "LKV", LKV_B)
                                    if ec == 4 and tile == 0:
                                        S.wait(t, "LX", LX_B)
                                    if tile > 0 and ec == 0:
                                        S.wait(t, "DV",
                                               ck[("B_phi", tile - 1, CC - 2)])
                                    if tile > 0 and ec == 1:
                                        S.wait(t, "DV",
                                               ck[("B_phi", tile - 1, CC - 1)])
                                    if ec >= 2:
                                        S.wait(t, "DV", ck[("B_phi", tile, ec - 2)])
                                    for cc in range(CC):
                                        mi = t.matmul(pq[ec % 2][:],
                                                      wq[:, cc, ec * 128:(ec + 1) * 128],
                                                      X[:, cc, tsl],
                                                      start=(cc == 0), stop=(cc == CC - 1))
                                    ck[("B_q", tile, ec)] = S.inc(mi, "PE")
                                block.tensor(_peq)

                                def _dvq1(v, ec=ec, tile=tile):
                                    S.wait(v, "PE", ck[("B_q", tile, ec)])
                                    prev = ck.get(("B_exp", tile, ec - 1)) if ec > 0 else \
                                        (ck.get(("B_exp", tile - 1, CC - 1)) if tile > 0 else 0)
                                    if prev:
                                        S.wait(v, "AC", prev)
                                    mi = v.tensor_scalar(sA2, pq[ec % 2][:], 0.0, None,
                                                         ALU.min)
                                    ck[("B_min", tile, ec)] = S.inc(mi, "DV")
                                block.vector(_dvq1)

                                def _acq(a, ec=ec, tile=tile):
                                    S.wait(a, "DV", ck[("B_min", tile, ec)])
                                    prev = ck.get(("B_phi", tile, ec - 1)) if ec > 0 else \
                                        (ck.get(("B_phi", tile - 1, CC - 1)) if tile > 0 else 0)
                                    if prev:
                                        S.wait(a, "DV", prev)
                                    mi = a.activation(sB2, sA2, AF.Exp)
                                    ck[("B_exp", tile, ec)] = S.inc(mi, "AC")
                                block.scalar(_acq)

                                def _dvq2(v, ec=ec, tile=tile):
                                    S.wait(v, "AC", ck[("B_exp", tile, ec)])
                                    mi = v.scalar_tensor_tensor(Qf[:, ec, :], pq[ec % 2][:],
                                                                0.0, sB2, ALU.max, ALU.add)
                                    ck[("B_phi", tile, ec)] = S.inc(mi, "DV")
                                block.vector(_dvq2)

                    if tile > 0:
                        _lnbx(tile - 1, mid=_mid_ps1)
                    else:
                        _mid_ps1()

                    with contextlib.ExitStack() as ps2:
                        def _pen(t, tile=tile):
                            if tile == 0:
                                S.wait(t, "CXB", 32)
                            S.wait(t, "DV", ck[("B_phi", tile, CC - 1)])
                            for ec in range(CC):
                                mi = t.matmul(pnorm[:], KS[:, ec, :], Qf[:, ec, :],
                                              start=(ec == 0), stop=(ec == CC - 1))
                            ck[("B_nrm", tile)] = S.inc(mi, "PE")
                        block.tensor(_pen)

                        def _dvn(v, tile=tile):
                            S.wait(v, "PE", ck[("B_nrm", tile)])
                            c = S.inc(v.tensor_scalar(nrm[:], pnorm[:], EPS_ATTN, None,
                                                      ALU.add), "DV")
                            S.wait(v, "DV", c)
                            mi = v.reciprocal(inv[:], nrm[:])
                            ck[("B_inv", tile)] = S.inc(mi, "DV")
                        block.vector(_dvn)

                    with contextlib.ExitStack() as ps3:
                        pbc = [ps3.enter_context(
                            nc.psum_tensor(f"pbc{tile}_{i}", [128, NT], F32))
                            for i in range(2)]
                        po = [ps3.enter_context(
                            nc.psum_tensor(f"po{tile}_{i}", [128, NT], F32))
                            for i in range(2)]
                        invb = [sA2, sB2]
                        for ec in range(CC):
                            def _peo(t, ec=ec, tile=tile):
                                if tile > 0 and ec == 0:
                                    S.wait(t, "SH", ck[("BX_st", tile - 1, NJ - 1)])
                                if ec == 0:
                                    S.wait(t, "DV", ck[("B_inv", tile)])
                                if ec >= 2:
                                    S.wait(t, "DV", ck[("B_att", tile, ec - 2)])
                                t.matmul(pbc[ec % 2][:], bc16[:, ec, :], inv[:],
                                         start=True, stop=True)
                                mi = t.matmul(po[ec % 2][:], kvd[:, ec, :],
                                              Qf[:, ec, :], start=True, stop=True)
                                ck[("B_o", tile, ec)] = S.inc(mi, "PE")
                            block.tensor(_peo)

                            def _aci(a, ec=ec, tile=tile):
                                S.wait(a, "PE", ck[("B_o", tile, ec)])
                                if ec >= 2:
                                    S.wait(a, "DV", ck[("B_att", tile, ec - 2)])
                                mi = a.copy(invb[ec % 2][:], pbc[ec % 2][:])
                                ck[("B_ic", tile, ec)] = S.inc(mi, "AC")
                            block.scalar(_aci)

                            def _dvo(v, ec=ec, tile=tile):
                                S.wait(v, "PE", ck[("B_o", tile, ec)])
                                S.wait(v, "AC", ck[("B_ic", tile, ec)])
                                mi = v.tensor_tensor(Af[:, ec, :], po[ec % 2][:],
                                                     invb[ec % 2][:], ALU.mult)
                                ck[("B_att", tile, ec)] = S.inc(mi, "DV")
                            block.vector(_dvo)

                    with contextlib.ExitStack() as ps4:
                        pp = [ps4.enter_context(
                            nc.psum_tensor(f"pp{tile}_{i}", [128, NT], F32))
                            for i in range(2)]
                        for oc in range(CC):
                            def _pep(t, oc=oc, tile=tile):
                                if tile == 0 and oc == 0:
                                    S.wait(t, "GP", GP_AB)
                                    S.wait(t, "GP", GP_INIT)
                                    S.wait(t, "LD", LD_B)
                                if oc == 0:
                                    S.wait(t, "DV", ck[("B_att", tile, CC - 1)])
                                if oc >= 2:
                                    S.wait(t, "DV", ck[("B_y1", tile, oc - 2)])
                                t.matmul(pp[oc % 2][:],
                                         bproj_sb[0:1, oc * 128:(oc + 1) * 128],
                                         onesr[:], start=True, stop=False)
                                for cc in range(CC):
                                    mi = t.matmul(pp[oc % 2][:],
                                                  wp[:, cc, oc * 128:(oc + 1) * 128],
                                                  Af[:, cc, :],
                                                  start=False, stop=(cc == CC - 1))
                                ck[("B_p", tile, oc)] = S.inc(mi, "PE")
                            block.tensor(_pep)

                            def _dvp(v, oc=oc, tile=tile):
                                S.wait(v, "PE", ck[("B_p", tile, oc)])
                                mi = v.tensor_tensor(X[:, oc, tsl], pp[oc % 2][:],
                                                     X[:, oc, tsl], ALU.add)
                                ck[("B_y1", tile, oc)] = S.inc(mi, "DV")
                            block.vector(_dvp)

                _lnbx(NTILE - 1)

        with contextlib.ExitStack() as gctx:
            ge = gctx.enter_context(nc.sbuf_tensor("ge", [128, 16, ER], F32))
            gwt = gctx.enter_context(nc.sbuf_tensor("gwt", [128, 16, ER], F32))
            gmask = gctx.enter_context(nc.sbuf_tensor("gmask", [128, 16, ER], F32))
            gs = gctx.enter_context(nc.sbuf_tensor("gs", [128, 16, 1], F32))
            gr = gctx.enter_context(nc.sbuf_tensor("gr", [128, 16, 1], F32))
            gm8 = gctx.enter_context(nc.sbuf_tensor("gm8", [128, 16, 8], F32))
            gated_tm = gctx.enter_context(nc.sbuf_tensor("gated_tm", [128, 16, 16], F32))
            msk = gctx.enter_context(nc.sbuf_tensor("msk", [128, 16], F32))
            s1t = gctx.enter_context(nc.sbuf_tensor("s1t", [128, 16], F32))
            sm1 = gctx.enter_context(nc.sbuf_tensor("sm1", [128, 16], F32))
            sm2 = gctx.enter_context(nc.sbuf_tensor("sm2", [128, 16], F32))
            cs_sb = gctx.enter_context(nc.sbuf_tensor("cs_sb", [1, 16], F32))
            cs_col = gctx.enter_context(nc.sbuf_tensor("cs_col", [16, 1], F32))
            ioff = gctx.enter_context(nc.sbuf_tensor("ioff", [16, 1], F32))
            offs_row = gctx.enter_context(nc.sbuf_tensor("offs_row", [1, 16], F32))
            pgt = [gctx.enter_context(nc.psum_tensor(f"pgt{i}", [128, ER], F32))
                   for i in range(2)]
            ptg = [gctx.enter_context(nc.psum_tensor(f"ptg{i}", [16, 128], F32))
                   for i in range(2)]
            pslot = gctx.enter_context(nc.psum_tensor("pslot", [128, 16], F32))
            pcs = gctx.enter_context(nc.psum_tensor("pcs", [1, 16], F32))
            paux = gctx.enter_context(nc.psum_tensor("paux", [16, 16], F32))

            for ch in range(16):
                def _peg(t, ch=ch):
                    if ch == 0:
                        S.wait(t, "DV", ck[("BX_ev", NTILE - 1, NJ * CC - 1)])
                    if ch >= 2:
                        S.wait(t, "AC", ck[("G_exp", ch - 2)])
                    for cc in range(CC):
                        t.matmul(pgt[ch % 2][:], X[:, cc, ch * 128:(ch + 1) * 128],
                                 gw_sb[:, cc, :], start=(cc == 0), stop=False)
                    mi = t.matmul(pgt[ch % 2][:], ones1[:], gb_sb[:],
                                  start=False, stop=True)
                    ck[("G_mm", ch)] = S.inc(mi, "PE")
                block.tensor(_peg)

                def _acg(a, ch=ch):
                    S.wait(a, "PE", ck[("G_mm", ch)])
                    mi = a.activation(ge[:, ch, :], pgt[ch % 2][:], AF.Exp)
                    ck[("G_exp", ch)] = S.inc(mi, "AC")
                block.scalar(_acg)

            def _dvg(v):
                S.wait(v, "AC", ck[("G_exp", 15)])
                v.memset(gated_tm[:, :, 15:16], 1.0)
                c = S.inc(v.tensor_reduce(gs[:], ge[:], AX.X, ALU.add), "DV")
                S.wait(v, "DV", c)
                c = S.inc(v.reciprocal(gr[:], gs[:]), "DV")
                S.wait(v, "DV", c)
                c = S.inc(v.tensor_tensor(gwt[:], ge[:],
                                          gr[:].to_broadcast([128, 16, ER]),
                                          ALU.mult), "DV")
                S.wait(v, "DV", c)
                for ch in range(16):
                    mi = v.max(out=gm8[:, ch, :], in_=gwt[:, ch, :])
                c = S.inc(mi, "DV")
                S.wait(v, "DV", c)
                c = S.inc(v.tensor_tensor(gmask[:], gwt[:],
                                          gm8[:, :, 2:3].to_broadcast([128, 16, ER]),
                                          ALU.is_ge), "DV")
                S.wait(v, "DV", c)
                mi = v.tensor_tensor(gated_tm[:, :, 0:ER], gwt[:], gmask[:], ALU.mult)
                ck["G_tm"] = S.inc(mi, "DV")
            block.vector(_dvg)

            for c16 in range(16):
                def _stg(sync, c16=c16):
                    S.wait(sync, "DV", ck["G_tm"])
                    mi = sync.dma_start(
                        out=x1h[c16 * 128:(c16 + 1) * 128, C:C + 16],
                        in_=gated_tm[:, c16, :])
                    ck[("G_st", c16)] = S.inc(mi, "SH", 16)
                block.sync(_stg)

            for ch in range(16):
                def _petr(t, ch=ch):
                    S.wait(t, "DV", ck["G_tm"])
                    if ch >= 2:
                        S.wait(t, "DV", ck[("G_fm", ch - 2)])
                    mi = t.transpose(ptg[ch % 2][:], gated_tm[:, ch, :], ident[:])
                    ck[("G_tr", ch)] = S.inc(mi, "PE")
                block.tensor(_petr)

                def _dvtr(v, ch=ch):
                    S.wait(v, "PE", ck[("G_tr", ch)])
                    mi = v.tensor_copy(gated_fm[:, ch * 128:(ch + 1) * 128],
                                       ptg[ch % 2][:])
                    ck[("G_fm", ch)] = S.inc(mi, "DV")
                block.vector(_dvtr)

            for e in range(ER):
                def _dvm(v, e=e):
                    S.wait(v, "DV", ck["G_tm"])
                    if e > 0:
                        S.wait(v, "DV", ck[("SL", e - 1)])
                    mi = v.tensor_scalar(msk[:], gated_tm[:, :, e], 0.0, None,
                                         ALU.is_gt)
                    ck[("SLm", e)] = S.inc(mi, "DV")
                block.vector(_dvm)

                def _pes1(t, e=e):
                    S.wait(t, "DV", ck[("SLm", e)])
                    t.matmul(pslot[:], ltri_sb[:], msk[:], start=True, stop=False)
                    mi = t.matmul(pcs[:], onesc[:], msk[:], start=True, stop=True)
                    ck[("SLa", e)] = S.inc(mi, "PE")
                block.tensor(_pes1)

                def _dvc1(v, e=e):
                    S.wait(v, "PE", ck[("SLa", e)])
                    mi = v.tensor_copy(cs_sb[:], pcs[:])
                    ck[("SLb", e)] = S.inc(mi, "DV")
                block.vector(_dvc1)

                def _pet1(t, e=e):
                    S.wait(t, "DV", ck[("SLb", e)])
                    mi = t.transpose(paux[:, 0:1], cs_sb[:], ident[0:1, 0:1])
                    ck[("SLc", e)] = S.inc(mi, "PE")
                block.tensor(_pet1)

                def _dvc2(v, e=e):
                    S.wait(v, "PE", ck[("SLc", e)])
                    mi = v.tensor_copy(cs_col[:], paux[:, 0:1])
                    ck[("SLd", e)] = S.inc(mi, "DV")
                block.vector(_dvc2)

                def _pet2(t, e=e):
                    S.wait(t, "DV", ck[("SLd", e)])
                    mi = t.matmul(paux[:, 1:2], ltri_sb[0:16, 0:16], cs_col[:],
                                  start=False, stop=False, skip_group_check=True)
                    ck[("SLe", e)] = S.inc(mi, "PE")
                block.tensor(_pet2)

                def _dvc3(v, e=e):
                    S.wait(v, "PE", ck[("SLe", e)])
                    mi = v.tensor_copy(ioff[:], paux[:, 1:2])
                    ck[("SLf", e)] = S.inc(mi, "DV")
                block.vector(_dvc3)

                def _pet3(t, e=e):
                    S.wait(t, "DV", ck[("SLf", e)])
                    mi = t.transpose(pcs[:], ioff[:], ident[0:16, 0:16])
                    ck[("SLg", e)] = S.inc(mi, "PE")
                block.tensor(_pet3)

                def _dvc4(v, e=e):
                    S.wait(v, "PE", ck[("SLg", e)])
                    mi = v.tensor_copy(offs_row[:], pcs[:])
                    ck[("SLh", e)] = S.inc(mi, "DV")
                block.vector(_dvc4)

                def _pet4(t, e=e):
                    S.wait(t, "DV", ck[("SLh", e)])
                    mi = t.matmul(pslot[:], ones1[:], offs_row[:],
                                  start=False, stop=True)
                    ck[("SLi", e)] = S.inc(mi, "PE")
                block.tensor(_pet4)

                def _dvc5(v, e=e):
                    S.wait(v, "PE", ck[("SLi", e)])
                    c = S.inc(v.tensor_scalar(sm1[:], pslot[:], 127.5, None,
                                              ALU.is_gt), "DV")
                    S.wait(v, "DV", c)
                    for kth in range(2, NCAP):
                        c = S.inc(v.tensor_scalar(sm2[:], pslot[:],
                                                  kth * 128 - 0.5, None,
                                                  ALU.is_gt), "DV")
                        S.wait(v, "DV", c)
                        c = S.inc(v.tensor_tensor(sm1[:], sm1[:], sm2[:], ALU.add),
                                  "DV")
                        S.wait(v, "DV", c)
                    c = S.inc(v.tensor_scalar(sm2[:], sm1[:], float(NCAP * 128 - 1),
                                              None, ALU.mult), "DV")
                    S.wait(v, "DV", c)
                    c = S.inc(v.scalar_tensor_tensor(s1t[:], pslot[:], float(NCAP),
                                                     sm2[:], ALU.mult, ALU.subtract),
                              "DV")
                    S.wait(v, "DV", c)
                    c = S.inc(v.tensor_scalar(sm2[:], pslot[:], CAPE - 0.5, None,
                                              ALU.is_gt), "DV")
                    S.wait(v, "DV", c)
                    c = S.inc(v.scalar_tensor_tensor(s1t[:], sm2[:], BIGF, s1t[:],
                                                     ALU.mult, ALU.add), "DV")
                    S.wait(v, "DV", c)
                    c = S.inc(v.scalar_tensor_tensor(sm2[:], msk[:], -BIGF, s1t[:],
                                                     ALU.mult, ALU.add), "DV")
                    S.wait(v, "DV", c)
                    c = S.inc(v.tensor_scalar(s1t[:], sm2[:], BIGF, None, ALU.add),
                              "DV")
                    S.wait(v, "DV", c)
                    mi = v.tensor_copy(slotsb[:, e, :], s1t[:])
                    ck[("SL", e)] = S.inc(mi, "DV")
                block.vector(_dvc5)
            ck["slots_done"] = ck[("SL", ER - 1)]

        with contextlib.ExitStack() as cctx:
            yac = cctx.enter_context(nc.sbuf_tensor("yac", [128, CC, TOWN], F32))
            he = cctx.enter_context(nc.sbuf_tensor("he", [128, NFC, NT], F32))
            w1r = cctx.enter_context(nc.sbuf_tensor("w1r", [128, NRING1, CC, 128], F32))
            w2r = cctx.enter_context(nc.sbuf_tensor("w2r", [128, NRING2, C], F32))
            otmC = cctx.enter_context(nc.sbuf_tensor("otmC", [128, 1, C], F32))

            g1_idx = [0]
            g2_idx = [0]
            first_c = [True]
            w1_last = {}
            w2_last = {}

            def _issue_w1(sync, src, sw):
                gi = g1_idx[0]
                if sw:
                    S.wait(sync, "DV", ck[("BX_ev", NTILE - 1, NJ * CC - 1)])
                    S.wait(sync, "SH", S.val("SH"))
                if gi >= NRING1:
                    S.wait(sync, "PE", w1_last[gi % NRING1])
                S.inc(sync.dma_start(out=w1r[:, gi % NRING1], in_=src),
                      f"LW1_{gi % NRING1}", 16)
                g1_idx[0] += 1

            def _issue_w2(gp, src, sw):
                gi = g2_idx[0]
                if sw:
                    S.wait(gp, "DV", ck[("BX_ev", NTILE - 1, NJ * CC - 1)])
                    S.wait(gp, "SH", S.val("SH"))
                if gi >= NRING2:
                    S.wait(gp, "PE", w2_last[gi % NRING2])
                S.inc(gp.dma_start(out=w2r[:, gi % NRING2], in_=src),
                      f"LW2_{gi % NRING2}", 16)
                g2_idx[0] += 1

            for tile in range(NTILE):
                tsl = slice(tile * NT, (tile + 1) * NT)
                for ei in range(NEXP):
                    idx = ei

                    with contextlib.ExitStack() as p1s:
                        ph = [p1s.enter_context(
                            nc.psum_tensor(f"ph{tile}_{ei}_{i}", [128, NT], F32))
                            for i in range(2)]

                        gexp0_w2 = [None]
                        for fc in range(NFC):
                            sw = first_c[0]
                            first_c[0] = False

                            def _i1(sync, idx=idx, fc=fc, sw=sw):
                                _issue_w1(sync, sw1_d[idx * NFC + fc], sw)
                            block.sync(_i1)

                            if fc == 0:
                                gexp0_w2[0] = g2_idx[0]

                                def _i2a(gp, idx=idx, sw=sw):
                                    _issue_w2(gp, sw2_d[idx * NFC + 0], sw)
                                    _issue_w2(gp, sw2_d[idx * NFC + 1], sw)
                                block.gpsimd(_i2a)

                            gi1 = g1_idx[0] - 1

                            def _pe1(t, tile=tile, ei=ei, fc=fc, gi1=gi1):
                                S.wait(t, f"LW1_{gi1 % NRING1}",
                                       (gi1 // NRING1 + 1) * 16)
                                if fc == 0:
                                    if tile == 0 and ei == 0:
                                        S.wait(t, "DV", ck[("G_fm", 15)])
                                        S.wait(t, "AC", ck[("G_exp", 15)])
                                        S.wait(t, "DV", ck["slots_done"])
                                    elif ei == 0:
                                        S.wait(t, "DV",
                                               ck[("D_ev", tile - 1, NJ * CC - 1)])
                                    else:
                                        S.wait(t, "DV", ck[("C_yev", tile, ei - 1)])
                                if fc >= 2:
                                    S.wait(t, "AC", ck[("C_h", tile, ei, fc - 2)])
                                for cc in range(CC):
                                    mi = t.matmul(ph[fc % 2][:],
                                                  w1r[:, gi1 % NRING1, cc, :],
                                                  X[:, cc, tsl],
                                                  start=(cc == 0), stop=(cc == CC - 1))
                                ck[("C_w1", tile, ei, fc)] = S.inc(mi, "PE")
                            block.tensor(_pe1)
                            w1_last[gi1 % NRING1] = ck[("C_w1", tile, ei, fc)]

                            def _ac1(a, tile=tile, ei=ei, idx=idx, fc=fc):
                                S.wait(a, "PE", ck[("C_w1", tile, ei, fc)])
                                bias = b1s_sb[:, idx * NFC + fc:idx * NFC + fc + 1]
                                mi = a.activation(he[:, fc, :], ph[fc % 2][:], AF.Gelu,
                                                  bias=bias)
                                ck[("C_h", tile, ei, fc)] = S.inc(mi, "AC")
                            block.scalar(_ac1)

                    with contextlib.ExitStack() as p2s:
                        py = [p2s.enter_context(
                            nc.psum_tensor(f"py{tile}_{ei}_{o}", [128, NT], F32))
                            for o in range(CC)]
                        gexp0 = gexp0_w2[0]

                        for fc in range(NFC):
                            gi2 = gexp0 + fc

                            def _pe2(t, tile=tile, ei=ei, fc=fc, gi2=gi2):
                                S.wait(t, f"LW2_{gi2 % NRING2}",
                                       (gi2 // NRING2 + 1) * 16)
                                if fc == 0:
                                    S.wait(t, "AC", ck[("C_h", tile, ei, NFC - 1)])
                                    if ei == 0:
                                        S.wait(t, "DV", ck[("G_fm", 15)])
                                        for oc in range(CC):
                                            t.matmul(py[oc][:],
                                                     b2_sb[:, oc * 128:(oc + 1) * 128],
                                                     gated_fm[:, tsl],
                                                     start=True, stop=False)
                                for oc in range(CC):
                                    mi = t.matmul(py[oc][:],
                                                  w2r[:, gi2 % NRING2,
                                                      oc * 128:(oc + 1) * 128],
                                                  he[:, fc, :],
                                                  start=(fc == 0 and ei != 0),
                                                  stop=(fc == NFC - 1))
                                ck[("C_w2f", tile, ei, fc)] = S.inc(mi, "PE")
                            block.tensor(_pe2)
                            w2_last[gi2 % NRING2] = ck[("C_w2f", tile, ei, fc)]

                            if fc + 2 < NFC:
                                def _i2b(gp, idx=idx, fc=fc):
                                    _issue_w2(gp, sw2_d[idx * NFC + fc + 2], False)
                                block.gpsimd(_i2b)

                        def _dvy(v, tile=tile, ei=ei):
                            S.wait(v, "PE", ck[("C_w2f", tile, ei, NFC - 1)])
                            if ei > 0:
                                S.wait(v, "DV", ck[("C_yev", tile, ei - 1)])
                            for oc in range(CC):
                                if ei == 0:
                                    mi = v.scalar_tensor_tensor(
                                        yac[:, oc, tsl], X[:, oc, tsl], 2.0,
                                        py[oc][:], ALU.mult, ALU.add)
                                else:
                                    mi = v.tensor_tensor(yac[:, oc, tsl],
                                                         yac[:, oc, tsl],
                                                         py[oc][:], ALU.add)
                            ck[("C_yev", tile, ei)] = S.inc(mi, "DV")
                        block.vector(_dvy)

                with contextlib.ExitStack() as pds:
                    ptx = [pds.enter_context(
                        nc.psum_tensor(f"ptx{tile}_{i}", [128, 128], F32))
                        for i in range(2)]
                    for j in range(NJ):
                        for oc in range(CC):
                            k = j * CC + oc

                            def _petx(t, tile=tile, j=j, oc=oc, k=k):
                                if k == 0:
                                    S.wait(t, "DV", ck[("C_yev", tile, NEXP - 1)])
                                if k >= 2:
                                    S.wait(t, "DV", ck[("D_ev", tile, k - 2)])
                                mi = t.transpose(
                                    ptx[k % 2][:],
                                    yac[:, oc,
                                        tile * NT + j * 128:tile * NT + (j + 1) * 128],
                                    ident[:])
                                ck[("D_tr", tile, k)] = S.inc(mi, "PE")
                            block.tensor(_petx)

                            def _dvtx(v, tile=tile, j=j, oc=oc, k=k):
                                S.wait(v, "PE", ck[("D_tr", tile, k)])
                                prev = ck.get(("D_st", tile, j - 1)) if j >= 1 else \
                                    (ck.get(("D_st", tile - 1, NJ - 1))
                                     if tile > 0 else 0)
                                if prev:
                                    S.wait(v, "SH", prev)
                                mi = v.tensor_copy(
                                    otmC[:, 0, oc * 128:(oc + 1) * 128],
                                    ptx[k % 2][:])
                                ck[("D_ev", tile, k)] = S.inc(mi, "DV")
                            block.vector(_dvtx)

                        def _st(sync, tile=tile, j=j):
                            S.wait(sync, "DV", ck[("D_ev", tile, j * CC + CC - 1)])
                            r0 = tile * NT + j * 128
                            mi = sync.dma_start(out=ybuf[r0:r0 + 128, :],
                                                in_=otmC[:, 0, :])
                            ck[("D_st", tile, j)] = S.inc(mi, "SH", 16)
                        block.sync(_st)
        SH_DENSE = S.val("SH")

        mctx.close()

        with contextlib.ExitStack() as rctx:
            g_tm = [rctx.enter_context(nc.sbuf_tensor(f"g_tm{i}", [128, NCAP, ROWW], F32))
                    for i in range(2)]
            idsb = rctx.enter_context(nc.sbuf_tensor("idsb", [128, ER, NCAP],
                                                     mybir.dt.uint32))
            Xe = [rctx.enter_context(nc.sbuf_tensor(f"Xe{i}", [128, CC, CAP], F32))
                  for i in range(2)]
            grow = [rctx.enter_context(nc.sbuf_tensor(f"grow{i}", [1, CAP], F32))
                    for i in range(2)]
            G_sb = [rctx.enter_context(nc.sbuf_tensor(f"G_sb{i}", [128, CAP], F32))
                    for i in range(2)]
            her = rctx.enter_context(nc.sbuf_tensor("her", [128, NFC, CAP], F32))
            w1b = rctx.enter_context(nc.sbuf_tensor("w1b", [128, NRING1, CC, 128], F32))
            w2b = rctx.enter_context(nc.sbuf_tensor("w2b", [128, NRING2, C], F32))
            idinit_row = rctx.enter_context(nc.sbuf_tensor("idinit_row", [1, CAP],
                                                           mybir.dt.uint32))

            def _ld_idinit(sync):
                S.wait(sync, "DV", ck[("D_ev", NTILE - 1, NJ * CC - 1)])
                S.wait(sync, "SH", SH_DENSE)
                S.inc(sync.dma_start(out=idinit_row[:], in_=idinit_d), "LD", 16)
            block.sync(_ld_idinit)
            LD_IDINIT = S.val("LD")

            r1_idx = [g1_idx[0]]
            r2_idx = [g2_idx[0]]
            r1_last = dict(w1_last)
            r2_last = dict(w2_last)

            def _issue_rw1(sync, src, first):
                gi = r1_idx[0]
                if first:
                    S.wait(sync, "DV", ck[("D_ev", NTILE - 1, NJ * CC - 1)])
                    S.wait(sync, "SH", SH_DENSE)
                if gi >= NRING1:
                    S.wait(sync, "PE", r1_last[gi % NRING1])
                S.inc(sync.dma_start(out=w1b[:, gi % NRING1], in_=src),
                      f"LW1_{gi % NRING1}", 16)
                r1_idx[0] += 1

            def _issue_rw2(gp, src, first):
                gi = r2_idx[0]
                if first:
                    S.wait(gp, "DV", ck[("D_ev", NTILE - 1, NJ * CC - 1)])
                    S.wait(gp, "SH", SH_DENSE)
                if gi >= NRING2:
                    S.wait(gp, "PE", r2_last[gi % NRING2])
                S.inc(gp.dma_start(out=w2b[:, gi % NRING2], in_=src),
                      f"LW2_{gi % NRING2}", 16)
                r2_idx[0] += 1

            regs = {}

            def _gp_regs(gp):
                rb2047 = gp.alloc_register()
                gp.reg_mov(rb2047, TOWN - 1)
                rb767 = gp.alloc_register()
                gp.reg_mov(rb767, CAP - 1)
                regs["b2047"] = rb2047
                regs["b767"] = rb767
            block.gpsimd(_gp_regs)

            def _gp_fetch(gp, e):
                par = e % 2
                if e >= 2:
                    S.wait(gp, "PE", ck[("R_trl", e - 2)])
                    S.wait(gp, "SC", ck[("R_sc", e - 2)])
                    S.wait(gp, "DV", ck[("R_otm", e - 2, NCAP * CC - 1)])
                if e == 0:
                    S.wait(gp, "SH", SH_DENSE)
                    S.wait(gp, "DV", ck[("D_ev", NTILE - 1, NJ * CC - 1)])
                    S.wait(gp, "DV", ck["slots_done"])
                    S.wait(gp, "LD", LD_IDINIT)
                mi = gp.dma_start(out=idh[e].rearrange("r o -> o (r)"),
                                  in_=idinit_row[:])
                c_init = S.inc(mi, "IGI", 16)
                S.wait(gp, "IGI", c_init)
                for c16 in range(16):
                    mi = gp.indirect_dma_start(
                        out=idh[e],
                        out_offset=bass.IndirectOffsetOnAxis(
                            slotsb[:, e, c16:c16 + 1], 0),
                        in_=tokid_sb[:, c16:c16 + 1], in_offset=None,
                        bounds_check=regs["b767"], oob_is_err=False)
                    c_sc = S.inc(mi, "IGS", 16)
                S.wait(gp, "IGS", c_sc)
                mi = gp.dma_start(out=idsb[:, e, :],
                                  in_=idh[e].rearrange("(p k) o -> p (k o)", p=128))
                ck[("R_relo", e)] = S.inc(mi, "IGR", 16)
                S.wait(gp, "IGR", ck[("R_relo", e)])
                gsem = f"IGG{par}"
                for k in range(NCAP):
                    mi = gp.indirect_dma_start(
                        out=g_tm[par][:, k, :], out_offset=None,
                        in_=x1h,
                        in_offset=bass.IndirectOffsetOnAxis(
                            idsb[:, e, k:k + 1], 0),
                        bounds_check=regs["b2047"], oob_is_err=False)
                    c_g = S.inc(mi, gsem, 16)
                ck[("R_gath", e)] = (gsem, c_g)

            def _emit_transposes(e, ptr2, pG):
                par = e % 2

                for k in range(NCAP):
                    for oc in range(CC):
                        kk = k * CC + oc

                        def _ptr(t, e=e, par=par, k=k, oc=oc, kk=kk):
                            if kk == 0:
                                S.wait(t, ck[("R_gath", e)][0], ck[("R_gath", e)][1])
                                if e >= 2:
                                    S.wait(t, "DV", ck[("R_xe", e - 2, NCAP * CC - 1)])
                            if kk >= 2:
                                S.wait(t, "DV", ck[("R_xe", e, kk - 2)])
                            mi = t.transpose(ptr2[kk % 2][:],
                                             g_tm[par][:, k, oc * 128:(oc + 1) * 128],
                                             ident[:])
                            ck[("R_xtr", e, kk)] = S.inc(mi, "PE")
                        block.tensor(_ptr)

                        def _dve(v, e=e, par=par, k=k, oc=oc, kk=kk):
                            S.wait(v, "PE", ck[("R_xtr", e, kk)])
                            mi = v.tensor_copy(Xe[par][:, oc, k * 128:(k + 1) * 128],
                                               ptr2[kk % 2][:])
                            ck[("R_xe", e, kk)] = S.inc(mi, "DV")
                        block.vector(_dve)

                for k in range(NCAP):
                    def _ptg2(t, e=e, par=par, k=k):
                        if k < 2:
                            S.wait(t, "DV", ck[("R_xe", e, NCAP * CC - 2 + k)])
                        if k >= 2:
                            S.wait(t, "DV", ck[("R_ge", e, k - 2)])
                        mi = t.transpose(ptr2[k % 2][0:1, :],
                                         g_tm[par][:, k, C + e:C + e + 1],
                                         ident[:])
                        ck[("R_gtr", e, k)] = S.inc(mi, "PE")
                    block.tensor(_ptg2)

                    def _dvg2(v, e=e, par=par, k=k):
                        S.wait(v, "PE", ck[("R_gtr", e, k)])
                        mi = v.tensor_copy(grow[par][0:1, k * 128:(k + 1) * 128],
                                           ptr2[k % 2][0:1, :])
                        ck[("R_ge", e, k)] = S.inc(mi, "DV")
                    block.vector(_dvg2)

                def _pgb(t, e=e, par=par):
                    S.wait(t, "DV", ck[("R_ge", e, NCAP - 1)])
                    for (s0, s1) in SUBS:
                        mi = t.matmul(pG[:, s0:s1], ones1[:],
                                      grow[par][0:1, s0:s1], start=True, stop=True)
                    ck[("R_pg", e)] = S.inc(mi, "PE")
                block.tensor(_pgb)

                def _acg2(a, e=e, par=par):
                    S.wait(a, "PE", ck[("R_pg", e)])
                    mi = a.copy(G_sb[par][:, 0:CAPE], pG[:, 0:CAPE])
                    ck[("R_gsb", e)] = S.inc(mi, "AC")
                block.scalar(_acg2)

                ck[("R_trl", e)] = ck[("R_pg", e)]

            def _gp_f0(gp):
                _gp_fetch(gp, 0)
            block.gpsimd(_gp_f0)
            with contextlib.ExitStack() as t0s:
                ptr2_0 = [t0s.enter_context(
                    nc.psum_tensor(f"rtr0_{i}", [128, 128], F32)) for i in range(2)]
                pG_0 = t0s.enter_context(nc.psum_tensor("rpg0", [128, CAP], F32))
                _emit_transposes(0, ptr2_0, pG_0)

            for e in range(ER):
                par = e % 2
                if e + 1 < ER:
                    def _gp_f(gp, e=e):
                        _gp_fetch(gp, e + 1)
                    block.gpsimd(_gp_f)

                with contextlib.ExitStack() as hs:
                    ph = [hs.enter_context(
                        nc.psum_tensor(f"rph{e}_{i}", [128, CAP], F32))
                        for i in range(2)]

                    for fc in range(NFC):
                        def _i1r(sync, e=e, fc=fc):
                            _issue_rw1(sync, rw1_d[e, fc], e == 0 and fc == 0)
                        block.sync(_i1r)
                        gi1 = r1_idx[0] - 1

                        def _rpe1(t, e=e, par=par, fc=fc, gi1=gi1):
                            S.wait(t, f"LW1_{gi1 % NRING1}", (gi1 // NRING1 + 1) * 16)
                            if fc == 0:
                                S.wait(t, "DV", ck[("R_xe", e, NCAP * CC - 1)])
                                S.wait(t, "DV", ck[("R_ge", e, NCAP - 1)])
                                S.wait(t, "AC", ck[("R_gsb", e)])
                                if e > 0:
                                    S.wait(t, "DV", ck[("R_yev", e - 1)])
                                    S.wait(t, "DV",
                                           ck[("R_otm", e - 1, NCAP * CC - 1)])
                            if fc >= 2:
                                S.wait(t, "AC", ck[("R_h", e, fc - 2)])
                            for (s0, s1) in SUBS:
                                for cc in range(CC):
                                    mi = t.matmul(ph[fc % 2][:, s0:s1],
                                                  w1b[:, gi1 % NRING1, cc, :],
                                                  Xe[par][:, cc, s0:s1],
                                                  start=(cc == 0), stop=(cc == CC - 1))
                            ck[("R_w1", e, fc)] = S.inc(mi, "PE")
                        block.tensor(_rpe1)
                        r1_last[gi1 % NRING1] = ck[("R_w1", e, fc)]

                        def _rac1(a, e=e, fc=fc):
                            S.wait(a, "PE", ck[("R_w1", e, fc)])
                            mi = a.activation(her[:, fc, 0:CAPE],
                                              ph[fc % 2][:, 0:CAPE], AF.Gelu,
                                              bias=b1r_sb[:, e, fc:fc + 1])
                            ck[("R_h", e, fc)] = S.inc(mi, "AC")
                        block.scalar(_rac1)

                    if e + 1 < ER:
                        with contextlib.ExitStack() as ts2:
                            ptr2_n = [ts2.enter_context(
                                nc.psum_tensor(f"rtr{e + 1}_{i}", [128, 128], F32))
                                for i in range(2)]
                            pG_n = ts2.enter_context(
                                nc.psum_tensor(f"rpg{e + 1}", [128, CAP], F32))
                            _emit_transposes(e + 1, ptr2_n, pG_n)

                with contextlib.ExitStack() as ys:
                    py = [ys.enter_context(
                        nc.psum_tensor(f"rpy{e}_{o}", [128, NT], F32))
                        for o in range(CC)]

                    for si, (s0, s1) in enumerate(SUBS):
                        sw_ = s1 - s0
                        for fc in range(NFC):
                            def _i2r(gp, e=e, fc=fc):
                                _issue_rw2(gp, rw2_d[e, fc],
                                           e == 0 and si == 0 and fc == 0)
                            block.gpsimd(_i2r)
                            gi2 = r2_idx[0] - 1

                            def _rpe2(t, e=e, par=par, si=si, s0=s0, s1=s1, sw_=sw_,
                                      fc=fc, gi2=gi2):
                                S.wait(t, f"LW2_{gi2 % NRING2}",
                                       (gi2 // NRING2 + 1) * 16)
                                if fc == 0 and si == 0:
                                    S.wait(t, "AC", ck[("R_h", e, NFC - 1)])
                                    S.wait(t, "AC", ck[("R_gsb", e)])
                                    if e + 1 < ER:
                                        S.wait(t, "AC", ck[("R_gsb", e + 1)])
                                        S.wait(t, "DV", ck[("R_ge", e + 1, NCAP - 1)])
                                        S.wait(t, "DV",
                                               ck[("R_xe", e + 1, NCAP * CC - 1)])
                                if fc == 0 and si == 1:
                                    S.wait(t, "DV", ck[("R_oev", e, 0)])
                                for oc in range(CC):
                                    mi = t.matmul(py[oc][:, 0:sw_],
                                                  w2b[:, gi2 % NRING2,
                                                      oc * 128:(oc + 1) * 128],
                                                  her[:, fc, s0:s1],
                                                  start=(fc == 0), stop=(fc == NFC - 1))
                                ck[("R_w2", e, si, fc)] = S.inc(mi, "PE")
                            block.tensor(_rpe2)
                            r2_last[gi2 % NRING2] = ck[("R_w2", e, si, fc)]

                        def _rdvy(v, e=e, par=par, si=si, s0=s0, s1=s1, sw_=sw_):
                            S.wait(v, "PE", ck[("R_w2", e, si, NFC - 1)])
                            for oc in range(CC):
                                mi = v.scalar_tensor_tensor(
                                    Xe[par][:, oc, s0:s1], py[oc][:, 0:sw_], 1.0,
                                    G_sb[par][:, s0:s1], ALU.mult, ALU.mult)
                            ck[("R_oev", e, si)] = S.inc(mi, "DV")
                        block.vector(_rdvy)
                    ck[("R_yev", e)] = ck[("R_oev", e, len(SUBS) - 1)]

                with contextlib.ExitStack() as os_:
                    ptxr = [os_.enter_context(
                        nc.psum_tensor(f"rptx{e}_{i}", [128, 128], F32))
                        for i in range(2)]
                    for k in range(NCAP):
                        for oc in range(CC):
                            kk = k * CC + oc

                            def _rptx(t, e=e, k=k, oc=oc, kk=kk):
                                if kk == 0:
                                    S.wait(t, "DV", ck[("R_yev", e)])
                                    if e > 0:
                                        S.wait(t, "SC", ck[("R_sc", e - 1)])
                                if kk >= 2:
                                    S.wait(t, "DV", ck[("R_otm", e, kk - 2)])
                                mi = t.transpose(ptxr[kk % 2][:],
                                                 Xe[e % 2][:, oc, k * 128:(k + 1) * 128],
                                                 ident[:])
                                ck[("R_otr", e, kk)] = S.inc(mi, "PE")
                            block.tensor(_rptx)

                            def _rdvt(v, e=e, k=k, oc=oc, kk=kk):
                                S.wait(v, "PE", ck[("R_otr", e, kk)])
                                mi = v.tensor_copy(
                                    g_tm[e % 2][:, k, oc * 128:(oc + 1) * 128],
                                    ptxr[kk % 2][:])
                                ck[("R_otm", e, kk)] = S.inc(mi, "DV")
                            block.vector(_rdvt)

                def _gp_sc(gp, e=e, par=par):
                    S.wait(gp, "DV", ck[("R_otm", e, NCAP * CC - 1)])
                    if e == 2 and DBG_E2:
                        gp.dma_start(out=dbg1, in_=g_tm[par][:]).then_inc(
                            S.h["SC"], 16)
                        S.v["SC"] += 16
                        gp.dma_start(out=dbg2, in_=G_sb[par][:]).then_inc(
                            S.h["SC"], 16)
                        S.v["SC"] += 16
                        gp.wait_ge(S.h["SC"], S.v["SC"])
                    if e == 0:
                        S.wait(gp, "SH", SH_DENSE)
                    else:
                        S.wait(gp, "SC", ck[("R_sc", e - 1)])
                    for k in range(NCAP):
                        mi = gp.indirect_dma_start(
                            out=ybuf,
                            out_offset=bass.IndirectOffsetOnAxis(
                                idsb[:, e, k:k + 1], 0),
                            in_=g_tm[par][:, k, 0:C], in_offset=None,
                            bounds_check=regs["b2047"], oob_is_err=False,
                            compute_op=ALU.add)
                        c_s = S.inc(mi, "SC", 16)
                    S.wait(gp, "SC", c_s)
                    ck[("R_sc", e)] = c_s
                    if e == 2 and DBG_E2:
                        gp.dma_start(out=dbg3, in_=ybuf[0:128, :]).then_inc(
                            S.h["SC"], 16)
                        S.v["SC"] += 16
                        gp.wait_ge(S.h["SC"], S.v["SC"])
                        ck[("R_sc", e)] = S.v["SC"]
                block.gpsimd(_gp_sc)
        SC_ALL = S.val("SC")

        with contextlib.ExitStack() as lctx:
            ysb = lctx.enter_context(nc.sbuf_tensor("ysb", [128, 16, C], F32))
            sqs = lctx.enter_context(nc.sbuf_tensor("sqs", [128, 2, C], F32))
            mcolv = lctx.enter_context(nc.sbuf_tensor("mcolv", [128, 16], F32))
            scolv = lctx.enter_context(nc.sbuf_tensor("scolv", [128, 16], F32))
            vcolv = lctx.enter_context(nc.sbuf_tensor("vcolv", [128, 16], F32))
            sdcv = lctx.enter_context(nc.sbuf_tensor("sdcv", [128, 16], F32))
            rscv = lctx.enter_context(nc.sbuf_tensor("rscv", [128, 16], F32))
            bcolv = lctx.enter_context(nc.sbuf_tensor("bcolv", [128, 16], F32))

            for c16 in range(16):
                sem_g = ["LX", "LD", "LW1_0", "LW1_1"][c16 % 4]

                def _ldy(sync, c16=c16, sem_g=sem_g):
                    if c16 == 0:
                        S.wait(sync, "SC", SC_ALL)
                    if c16 >= 4:
                        S.wait(sync, "AC", ck[("L2s2", c16 - 4)])
                        S.wait(sync, "DV", ck[("L2s1", c16 - 4)])
                    mi = sync.dma_start(out=ysb[:, c16, :],
                                        in_=ybuf[c16 * 128:(c16 + 1) * 128, :])
                    ck[("L2ld", c16)] = S.inc(mi, sem_g, 16)
                block.sync(_ldy)

                def _dvl1(v, c16=c16, sem_g=sem_g):
                    S.wait(v, sem_g, ck[("L2ld", c16)])
                    mi = v.tensor_reduce(mcolv[:, c16:c16 + 1], ysb[:, c16, :],
                                         AX.X, ALU.add)
                    ck[("L2s1", c16)] = S.inc(mi, "DV")
                block.vector(_dvl1)

                def _acl(a, c16=c16, sem_g=sem_g):
                    S.wait(a, sem_g, ck[("L2ld", c16)])
                    if c16 >= 2:
                        S.wait(a, "AC", ck[("L2s2", c16 - 2)])
                    mi = a.activation(sqs[:, c16 % 2, :], ysb[:, c16, :],
                                      AF.Square,
                                      accum_out=scolv[:, c16:c16 + 1])
                    ck[("L2s2", c16)] = S.inc(mi, "AC")
                block.scalar(_acl)

            def _dvb(v):
                S.wait(v, "AC", ck[("L2s2", 15)])
                S.wait(v, "DV", ck[("L2s1", 15)])
                c = S.inc(v.tensor_scalar(mcolv[:], mcolv[:], 1.0 / C, None,
                                          ALU.mult), "DV")
                S.wait(v, "DV", c)
                c = S.inc(v.tensor_scalar(scolv[:], scolv[:], 1.0 / C, None,
                                          ALU.mult), "DV")
                S.wait(v, "DV", c)
                c = S.inc(v.tensor_tensor(vcolv[:], mcolv[:], mcolv[:], ALU.mult),
                          "DV")
                S.wait(v, "DV", c)
                c = S.inc(v.tensor_tensor(vcolv[:], scolv[:], vcolv[:],
                                          ALU.subtract), "DV")
                S.wait(v, "DV", c)
                c = S.inc(v.tensor_scalar(vcolv[:], vcolv[:], EPS_LN, None,
                                          ALU.add), "DV")
                ck["L2v"] = c
            block.vector(_dvb)

            def _acb(a):
                S.wait(a, "DV", ck["L2v"])
                mi = a.activation(sdcv[:], vcolv[:], AF.Sqrt)
                ck["L2sd"] = S.inc(mi, "AC")
            block.scalar(_acb)

            def _dvb2(v):
                S.wait(v, "AC", ck["L2sd"])
                c = S.inc(v.reciprocal(rscv[:], sdcv[:]), "DV")
                S.wait(v, "DV", c)
                c = S.inc(v.tensor_tensor(bcolv[:], mcolv[:], rscv[:], ALU.mult),
                          "DV")
                S.wait(v, "DV", c)
                mi = v.tensor_scalar(bcolv[:], bcolv[:], -1.0, None, ALU.mult)
                ck["L2r"] = S.inc(mi, "DV")
            block.vector(_dvb2)

            for c16 in range(16):
                def _aca(a, c16=c16):
                    S.wait(a, "DV", ck["L2r"])
                    if c16 >= 2:
                        S.wait(a, f"ST{c16 % 2}", ck[("L2st", c16 - 2)])
                    mi = a.activation(sqs[:, c16 % 2, :], ysb[:, c16, :],
                                      AF.Identity,
                                      bias=bcolv[:, c16:c16 + 1],
                                      scale=rscv[:, c16:c16 + 1])
                    ck[("L2ap", c16)] = S.inc(mi, "AC")
                block.scalar(_aca)

                def _sto(sync, c16=c16):
                    S.wait(sync, "AC", ck[("L2ap", c16)])
                    mi = sync.dma_start(out=out[c16 * 128:(c16 + 1) * 128, :],
                                        in_=sqs[:, c16 % 2, :])
                    ck[("L2st", c16)] = S.inc(mi, f"ST{c16 % 2}", 16)
                block.sync(_sto)

            def _fin(sync):
                sync.wait_ge(S.h["ST0"], S.val("ST0"))
                sync.wait_ge(S.h["ST1"], S.val("ST1"))
            block.sync(_fin)

    return nc


def _emit_ln(nc, block, S, ck, tag, y_ap, sq_ap, dst_ap, dep_dv, dep_pe_sq,
             onesc, ones1, R0, R1, R2, R3, R4, sq_rotate, mid=None):
    phase, tile = tag
    with contextlib.ExitStack() as ps:
        ps1 = ps.enter_context(nc.psum_tensor(f"ps1{phase}{tile}", [1, NT], F32))
        ps2 = ps.enter_context(nc.psum_tensor(f"ps2{phase}{tile}", [1, NT], F32))
        pA = ps.enter_context(nc.psum_tensor(f"pA{phase}{tile}", [128, NT], F32))
        pB = ps.enter_context(nc.psum_tensor(f"pB{phase}{tile}", [128, NT], F32))

        for oc in range(CC):
            def _acs(a, oc=oc):
                S.wait(a, "DV", dep_dv(oc))
                if oc == 0:
                    S.wait(a, "PE", dep_pe_sq)
                if sq_rotate and oc >= 2:
                    S.wait(a, "PE", ck[("LN", phase, tile, "s2", oc - 2)])
                mi = a.activation(sq_ap(oc), y_ap(oc), AF.Square)
                ck[("LN", phase, tile, "sq", oc)] = S.inc(mi, "AC")
            block.scalar(_acs)

            def _pes(t, oc=oc):
                if oc == 0:
                    S.wait(t, "DV", dep_dv(CC - 1))
                t.matmul(ps1[:], onesc[:], y_ap(oc), start=(oc == 0),
                         stop=(oc == CC - 1))
                S.wait(t, "AC", ck[("LN", phase, tile, "sq", oc)])
                mi = t.matmul(ps2[:], onesc[:], sq_ap(oc), start=(oc == 0),
                              stop=(oc == CC - 1))
                ck[("LN", phase, tile, "s2", oc)] = S.inc(mi, "PE")
            block.tensor(_pes)

        if mid is not None:
            mid()

        def _dvs(v):
            S.wait(v, "PE", ck[("LN", phase, tile, "s2", CC - 1)])
            v.tensor_scalar(R0[:], ps1[:], 1.0 / C, None, ALU.mult)
            c = S.inc(v.tensor_scalar(R1[:], ps2[:], 1.0 / C, None, ALU.mult), "DV")
            S.wait(v, "DV", c)
            c = S.inc(v.tensor_tensor(R2[:], R0[:], R0[:], ALU.mult), "DV")
            S.wait(v, "DV", c)
            c = S.inc(v.tensor_tensor(R1[:], R1[:], R2[:], ALU.subtract), "DV")
            S.wait(v, "DV", c)
            mi = v.tensor_scalar(R1[:], R1[:], EPS_LN, None, ALU.add)
            ck[("LN", phase, tile, "var")] = S.inc(mi, "DV")
        block.vector(_dvs)

        def _acr(a):
            S.wait(a, "DV", ck[("LN", phase, tile, "var")])
            mi = a.activation(R3[:], R1[:], AF.Sqrt)
            ck[("LN", phase, tile, "std")] = S.inc(mi, "AC")
        block.scalar(_acr)

        def _dva(v):
            S.wait(v, "AC", ck[("LN", phase, tile, "std")])
            c = S.inc(v.reciprocal(R4[:], R3[:]), "DV")
            S.wait(v, "DV", c)
            mi = v.scalar_tensor_tensor(R2[:], R0[:], -1.0, R4[:], ALU.mult, ALU.mult)
            ck[("LN", phase, tile, "AB")] = S.inc(mi, "DV")
        block.vector(_dva)

        def _peb(t):
            S.wait(t, "DV", ck[("LN", phase, tile, "AB")])
            t.matmul(pA[:], ones1[:], R4[:], start=True, stop=True)
            mi = t.matmul(pB[:], ones1[:], R2[:], start=True, stop=True)
            ck[("LN", phase, tile, "bc")] = S.inc(mi, "PE")
        block.tensor(_peb)

        def _dvap(v):
            S.wait(v, "PE", ck[("LN", phase, tile, "bc")])
            for oc in range(CC):
                mi = v.tensor_tensor(y_ap(oc), y_ap(oc), pA[:], ALU.mult)
            c = S.inc(mi, "DV")
            S.wait(v, "DV", c)
            for oc in range(CC):
                mi = v.tensor_tensor(dst_ap(oc), y_ap(oc), pB[:], ALU.add)
            ck[("LN", phase, tile, "app2")] = S.inc(mi, "DV")
        block.vector(_dvap)




_NC_CACHE = {}


def _get_nc():
    if "nc" not in _NC_CACHE:
        _NC_CACHE["nc"] = build_nc()
    return _NC_CACHE["nc"]


def _prep_weights(Wqkv, Wproj, bproj, sW1, sb1, sW2, sb2, rW1, rb1, rW2, rb2,
                  gW, gb):
    f = np.float32
    d = {}
    d["wqkvT"] = np.ascontiguousarray(np.asarray(Wqkv, f).T)
    d["wprojT"] = np.ascontiguousarray(np.asarray(Wproj, f).T)
    d["bproj"] = np.asarray(bproj, f).reshape(1, C)
    d["gw"] = np.ascontiguousarray(np.asarray(gW, f))
    d["gb"] = np.asarray(gb, f).reshape(1, ER)
    sW1 = np.asarray(sW1, f)[0]
    d["sw1"] = np.ascontiguousarray(
        sW1.reshape(CC, 128, NFC_S, 128).transpose(2, 1, 0, 3))
    d["sw2"] = np.ascontiguousarray(np.asarray(sW2, f)[0].reshape(NFC_S, 128, C))
    rW1 = np.asarray(rW1, f)
    d["rw1"] = np.ascontiguousarray(
        rW1.reshape(ER, CC, 128, NFC, 128).transpose(0, 3, 2, 1, 4))
    d["rw2"] = np.ascontiguousarray(np.asarray(rW2, f).reshape(ER, NFC, 128, C))
    d["b1r"] = np.ascontiguousarray(
        np.asarray(rb1, f).reshape(ER, NFC, 128).transpose(2, 0, 1))
    d["b1s"] = np.ascontiguousarray(np.asarray(sb1, f)[0].reshape(NFC_S, 128).T)
    d["b2cat"] = np.ascontiguousarray(
        np.concatenate([np.asarray(rb2, f), np.asarray(sb2, f)], axis=0))
    d["ident"] = np.eye(128, dtype=f)
    d["ltri"] = np.triu(np.ones((128, 128), f), 1)
    d["tokid"] = np.ascontiguousarray(
        (np.arange(16)[None, :] * 128 + np.arange(128)[:, None]).astype(np.uint32))
    d["idinit"] = np.full((1, CAP), int(BIGF), np.uint32)
    bc = np.zeros((16, HD, 128), f)
    for ec in range(HD):
        bc[2 * ec, ec, 0:64] = 1.0
        bc[2 * ec + 1, ec, 64:128] = 1.0
    d["bc16"] = bc
    return d


def make_in_maps(inputs):
    x = np.asarray(inputs["L0_attn_x"], np.float32)
    w = _prep_weights(inputs["L0_attn_Wqkv"], inputs["L0_attn_Wproj"],
                      inputs["L0_attn_bproj"], inputs["L0_ffn_sW1"],
                      inputs["L0_ffn_sb1"], inputs["L0_ffn_sW2"],
                      inputs["L0_ffn_sb2"], inputs["L0_ffn_rW1"],
                      inputs["L0_ffn_rb1"], inputs["L0_ffn_rW2"],
                      inputs["L0_ffn_rb2"], inputs["L0_ffn_gW"],
                      inputs["L0_ffn_gb"])
    in_maps = []
    for core in range(8):
        seq = core // 4
        o0 = (core % 4) * TOWN
        m = dict(w)
        m["xT"] = np.ascontiguousarray(x[seq][o0:o0 + TOWN].T)
        in_maps.append(m)
    return in_maps


def kernel(**inputs):
    in_maps = make_in_maps(inputs)
    nc = _get_nc()
    res = run_bass_kernel_spmd(nc, in_maps, core_ids=list(range(8)))
    full = np.empty((2, TSEQ, C), np.float32)
    for core in range(8):
        seq = core // 4
        o0 = (core % 4) * TOWN
        full[seq, o0:o0 + TOWN] = res.results[core]["out"]
    return full


# version 15
# speedup vs baseline: 1.0189x; 1.0189x over previous
import contextlib
import numpy as np

import concourse.bass as bass
import concourse.mybir as mybir
from concourse.bass_utils import run_bass_kernel_spmd

F32 = mybir.dt.float32
AF = mybir.ActivationFunctionType
ALU = mybir.AluOpType
AX = mybir.AxisListType

C = 1024
CC = 8
H = 16
HD = 8
TSEQ = 8192
TOWN = 2048
NT = 512
NTILE = TOWN // NT
NCH = TOWN // 128
GSZ = 256
NGRP = TSEQ // GSZ
ER = 15
FR = 2048
FS = 4096
NFC = 16
NFC_S = FS // 128
EPS_ATTN = 1e-6
EPS_LN = 1e-5
NRING1 = 2
NRING2 = 2
NEXP = 2
NJ = NT // 128
CAP = 512
NCAP = CAP // 128
ROWW = C + 16
BIGF = 4096.0
SUBS = [(0, 512)]
CAPE = SUBS[-1][1]
DBG_E2 = False


class Sems:
    def __init__(self, nc, ctx, names):
        self.h = {n: ctx.enter_context(nc.semaphore(n)) for n in names}
        self.v = {n: 0 for n in names}

    def inc(self, inst, name, k=1):
        inst.then_inc(self.h[name], k)
        self.v[name] += k
        return self.v[name]

    def val(self, name):
        return self.v[name]

    def wait(self, eng, name, val):
        if val > 0:
            eng.wait_ge(self.h[name], val)


def build_nc():
    nc = bass.Bass("TRN2", target_bir_lowering=False)

    xT = nc.dram_tensor("xT", [C, TOWN], F32, kind="ExternalInput").ap()
    wqkvT = nc.dram_tensor("wqkvT", [C, 3 * C], F32, kind="ExternalInput").ap()
    wprojT_d = nc.dram_tensor("wprojT", [C, C], F32, kind="ExternalInput").ap()
    bproj_d = nc.dram_tensor("bproj", [1, C], F32, kind="ExternalInput").ap()
    gw_d = nc.dram_tensor("gw", [C, ER], F32, kind="ExternalInput").ap()
    gb_d = nc.dram_tensor("gb", [1, ER], F32, kind="ExternalInput").ap()
    sw1_d = nc.dram_tensor("sw1", [NFC_S, 128, CC, 128], F32, kind="ExternalInput").ap()
    sw2_d = nc.dram_tensor("sw2", [NFC_S, 128, C], F32, kind="ExternalInput").ap()
    rw1_d = nc.dram_tensor("rw1", [ER, NFC, 128, CC, 128], F32, kind="ExternalInput").ap()
    rw2_d = nc.dram_tensor("rw2", [ER, NFC, 128, C], F32, kind="ExternalInput").ap()
    b1r_d = nc.dram_tensor("b1r", [128, ER, NFC], F32, kind="ExternalInput").ap()
    b1s_d = nc.dram_tensor("b1s", [128, NFC_S], F32, kind="ExternalInput").ap()
    b2cat_d = nc.dram_tensor("b2cat", [16, C], F32, kind="ExternalInput").ap()
    ident_d = nc.dram_tensor("ident", [128, 128], F32, kind="ExternalInput").ap()
    bc16_d = nc.dram_tensor("bc16", [16, HD, 128], F32, kind="ExternalInput").ap()
    ltri_d = nc.dram_tensor("ltri", [128, 128], F32, kind="ExternalInput").ap()
    tokid_d = nc.dram_tensor("tokid", [128, 16], mybir.dt.uint32, kind="ExternalInput").ap()
    idinit_d = nc.dram_tensor("idinit", [1, CAP], mybir.dt.uint32, kind="ExternalInput").ap()
    out = nc.dram_tensor("out", [TOWN, C], F32, kind="ExternalOutput").ap()
    x1h = nc.dram_tensor("x1h", [TOWN, ROWW], F32, kind="Internal").ap()
    ybuf = nc.dram_tensor("ybuf", [TOWN, C], F32, kind="Internal").ap()
    idh = [nc.dram_tensor(f"idh{i}", [CAP, 1], mybir.dt.uint32, kind="Internal").ap()
           for i in range(ER)]
    kvar_i = nc.dram_tensor("kvar_i", [128, HD * 128 + HD * 16], F32,
                            kind="Internal").ap()
    kvar_o = nc.dram_tensor("kvar_o", [128, HD * 128 + HD * 16], F32,
                            kind="Internal").ap()
    dbg1 = nc.dram_tensor("dbg1", [128, NCAP, ROWW], F32, kind="Internal").ap()
    dbg2 = nc.dram_tensor("dbg2", [128, CAP], F32, kind="Internal").ap()
    dbg3 = nc.dram_tensor("dbg3", [128, C], F32, kind="Internal").ap()

    xT_r = xT.rearrange("(cc p) t -> p cc t", p=128)
    wqkvT_r = wqkvT.rearrange("(cc p) e -> p cc e", p=128)
    wprojT_r = wprojT_d.rearrange("(cc p) e -> p cc e", p=128)
    gw_r = gw_d.rearrange("(cc p) e -> p cc e", p=128)

    ctx = contextlib.ExitStack()
    with ctx:
        S = Sems(nc, ctx, ["LD", "LX", "LW1_0", "LW1_1", "LW2_0", "LW2_1",
                           "PE", "DV", "AC", "GP", "ST0", "ST1", "SH", "SC",
                           "IGI", "IGS", "IGR", "IGG0", "IGG1", "CX", "CCOL", "CXB", "LKV"])
        ck = {}

        ident = ctx.enter_context(nc.sbuf_tensor("identt", [128, 128], F32))
        ones1 = ctx.enter_context(nc.sbuf_tensor("ones1", [1, 128], F32))
        ones16 = ctx.enter_context(nc.sbuf_tensor("ones16", [16, 128], F32))
        onesc = ctx.enter_context(nc.sbuf_tensor("onesc", [128, 1], F32))
        b1r_sb = ctx.enter_context(nc.sbuf_tensor("b1r_sb", [128, ER, NFC], F32))
        b1s_sb = ctx.enter_context(nc.sbuf_tensor("b1s_sb", [128, NFC_S], F32))
        b2_sb = ctx.enter_context(nc.sbuf_tensor("b2_sb", [16, C], F32))
        gw_sb = ctx.enter_context(nc.sbuf_tensor("gw_sb", [128, CC, ER], F32))
        gb_sb = ctx.enter_context(nc.sbuf_tensor("gb_sb", [1, ER], F32))
        ltri_sb = ctx.enter_context(nc.sbuf_tensor("ltri_sb", [128, 128], F32))
        tokid_sb = ctx.enter_context(nc.sbuf_tensor("tokid_sb", [128, 16],
                                                    mybir.dt.uint32))
        slotsb = ctx.enter_context(nc.sbuf_tensor("slotsb", [128, ER, 16],
                                                  mybir.dt.uint32))
        block = ctx.enter_context(nc.Block())

        mctx = contextlib.ExitStack()
        X = mctx.enter_context(nc.sbuf_tensor("X", [128, CC, TOWN], F32))
        gated_fm = mctx.enter_context(nc.sbuf_tensor("gated_fm", [16, TOWN], F32))
        R0 = mctx.enter_context(nc.sbuf_tensor("R0", [1, NT], F32))
        R1 = mctx.enter_context(nc.sbuf_tensor("R1", [1, NT], F32))
        R2 = mctx.enter_context(nc.sbuf_tensor("R2", [1, NT], F32))
        R3 = mctx.enter_context(nc.sbuf_tensor("R3", [1, NT], F32))
        R4 = mctx.enter_context(nc.sbuf_tensor("R4", [1, NT], F32))

        def _init_loads(sync):
            for dst, src in [
                (X[:, :, 0:GSZ], xT_r[:, :, 0:GSZ]),
                (ident[:], ident_d),
                (b1r_sb[:], b1r_d),
                (b1s_sb[:], b1s_d),
                (b2_sb[:], b2cat_d),
                (gw_sb[:], gw_r),
                (gb_sb[:], gb_d),
                (ltri_sb[:], ltri_d),
                (tokid_sb[:], tokid_d),
            ]:
                S.inc(sync.dma_start(out=dst, in_=src), "LD", 16)
        block.sync(_init_loads)
        LD_ALL = S.val("LD")

        def _init_memset(gp):
            S.inc(gp.memset(ones1[:], 1.0), "GP", 1)
            S.inc(gp.memset(ones16[:], 1.0), "GP", 1)
            S.inc(gp.memset(onesc[:], 1.0), "GP", 1)
        block.gpsimd(_init_memset)
        GP_INIT = S.val("GP")

        with contextlib.ExitStack() as abctx:
            kvd = abctx.enter_context(nc.sbuf_tensor("kvd", [128, HD, 128], F32))
            KS = abctx.enter_context(nc.sbuf_tensor("KS", [128, HD, 16], F32))
            bc16 = abctx.enter_context(nc.sbuf_tensor("bc16s", [16, HD, 128], F32))
            bproj_sb = abctx.enter_context(nc.sbuf_tensor("bproj_sb", [1, C], F32))
            onesr = abctx.enter_context(nc.sbuf_tensor("onesr", [1, NT], F32))

            def _ab_loads(sync):
                S.inc(sync.dma_start(out=bc16[:], in_=bc16_d), "LD", 16)
                S.inc(sync.dma_start(out=bproj_sb[:], in_=bproj_d), "LD", 16)
            block.sync(_ab_loads)

            def _ab_memset(gp):
                S.inc(gp.memset(onesr[:], 1.0), "GP", 1)
                S.inc(gp.memset(kvd[:], 0.0), "GP", 1)
                S.inc(gp.memset(KS[:], 0.0), "GP", 1)
            block.gpsimd(_ab_memset)
            GP_AB = S.val("GP")

            with contextlib.ExitStack() as actx:
                wkv = actx.enter_context(nc.sbuf_tensor("wkv", [128, CC, 2048], F32))
                Ktm = [actx.enter_context(nc.sbuf_tensor(f"Ktm{i}", [128, 1024], F32))
                       for i in range(2)]
                Vxt = [actx.enter_context(nc.sbuf_tensor(f"Vxt{i}", [128, HD, 129], F32))
                       for i in range(2)]
                sA = actx.enter_context(nc.sbuf_tensor("sA", [128, 1024], F32))
                sB = actx.enter_context(nc.sbuf_tensor("sB", [128, 1024], F32))

                pk = actx.enter_context(nc.psum_tensor("pk", [128, 1024], F32))
                pv = actx.enter_context(nc.psum_tensor("pv", [128, 1024], F32))
                pkv = [actx.enter_context(
                    nc.psum_tensor(f"pkv{i}", [128, 387 if i < 2 else 258], F32))
                    for i in range(3)]

                def kvreg(hd):
                    t_, j_ = divmod(hd, 3)
                    return pkv[t_][:, j_ * 129:(j_ + 1) * 129]

                def _vx_ones(gp):
                    for b in range(2):
                        S.inc(gp.memset(Vxt[b][:, :, 128:129], 1.0), "GP", 1)
                block.gpsimd(_vx_ones)
                GP_VX = S.val("GP")

                def _load_wkv(sync):
                    S.inc(sync.dma_start(out=wkv[:], in_=wqkvT_r[:, :, C:3 * C]),
                          "LKV", 16)
                    S.inc(sync.dma_start(out=X[:, :, GSZ:TOWN],
                                         in_=xT_r[:, :, GSZ:TOWN]), "LX", 16)
                block.sync(_load_wkv)

                first_bank = [True, True, True]

                def _kv_mms(t, j):
                    S.wait(t, "DV", ck[("A_vcp", j)])
                    for hd in range(HD):
                        st = first_bank[hd // 3]
                        if st:
                            first_bank[hd // 3] = False
                        mi = t.matmul(kvreg(hd),
                                      Ktm[j % 2][:, hd * 128:(hd + 1) * 128],
                                      Vxt[j % 2][:, hd, :],
                                      start=st, stop=(j == NCH - 1),
                                      skip_group_check=True)
                    ck[("A_kv", j)] = S.inc(mi, "PE")

                for j in range(NCH):
                        def _pe(t, j=j):
                            if j == 0:
                                S.wait(t, "LD", S.val("LD"))
                                S.wait(t, "LKV", S.val("LKV"))
                            if j == 2:
                                S.wait(t, "LX", S.val("LX"))
                            if j > 0:
                                S.wait(t, "DV", ck[("A_stt", j - 1)])
                            for cc in range(CC):
                                for hb in range(2):
                                    mi = t.matmul(
                                        pk[:, hb * 512:(hb + 1) * 512],
                                        X[:, cc, j * 128:(j + 1) * 128],
                                        wkv[:, cc, hb * 512:(hb + 1) * 512],
                                        start=(cc == 0), stop=(cc == CC - 1))
                            ck[("A_K", j)] = S.inc(mi, "PE")
                            if j > 0:
                                S.wait(t, "DV", ck[("A_vcp", j - 1)])
                            for cc in range(CC):
                                for hb in range(2):
                                    mi = t.matmul(
                                        pv[:, hb * 512:(hb + 1) * 512],
                                        X[:, cc, j * 128:(j + 1) * 128],
                                        wkv[:, cc, 1024 + hb * 512:1024 + (hb + 1) * 512],
                                        start=(cc == 0), stop=(cc == CC - 1))
                            ck[("A_V", j)] = S.inc(mi, "PE")
                            if j > 0:
                                _kv_mms(t, j - 1)
                        block.tensor(_pe)

                        def _dv1(v, j=j):
                            S.wait(v, "PE", ck[("A_K", j)])
                            if j > 0:
                                S.wait(v, "AC", ck[("A_exp", j - 1)])
                            mi = v.tensor_scalar(sA[:], pk[:], 0.0, None, ALU.min)
                            ck[("A_min", j)] = S.inc(mi, "DV")
                        block.vector(_dv1)

                        def _ac(a, j=j):
                            S.wait(a, "DV", ck[("A_min", j)])
                            if j > 0:
                                S.wait(a, "DV", ck[("A_stt", j - 1)])
                            mi = a.activation(sB[:], sA[:], AF.Exp)
                            ck[("A_exp", j)] = S.inc(mi, "AC")
                        block.scalar(_ac)

                        def _dv2(v, j=j):
                            S.wait(v, "AC", ck[("A_exp", j)])
                            if j > 1:
                                S.wait(v, "PE", ck[("A_kv", j - 2)])
                            mi = v.scalar_tensor_tensor(Ktm[j % 2][:], pk[:], 0.0,
                                                        sB[:], ALU.max, ALU.add)
                            ck[("A_stt", j)] = S.inc(mi, "DV")
                            S.wait(v, "PE", ck[("A_V", j)])
                            if j == 0:
                                S.wait(v, "GP", GP_VX)
                            mi = v.tensor_copy(
                                Vxt[j % 2][:, :, 0:128],
                                pv[:].rearrange("p (hd m) -> p hd m", m=128))
                            ck[("A_vcp", j)] = S.inc(mi, "DV")
                        block.vector(_dv2)

                def _pe_last(t):
                    _kv_mms(t, NCH - 1)
                block.tensor(_pe_last)

                def _dv_fin(v):
                    S.wait(v, "PE", ck[("A_kv", NCH - 1)])
                    S.wait(v, "GP", GP_AB)
                    for hd in range(HD):
                        reg = kvreg(hd)
                        v.tensor_copy(kvd[0:64, hd, 0:64], reg[0:64, 0:64])
                        v.tensor_copy(kvd[64:128, hd, 64:128], reg[64:128, 64:128])
                        v.tensor_copy(KS[0:64, hd, 2 * hd:2 * hd + 1],
                                      reg[0:64, 128:129])
                        mi = v.tensor_copy(KS[64:128, hd, 2 * hd + 1:2 * hd + 2],
                                           reg[64:128, 128:129])
                    ck["A_fin"] = S.inc(mi, "DV")
                block.vector(_dv_fin)

            with contextlib.ExitStack() as bctx:
                wq = bctx.enter_context(nc.sbuf_tensor("wq", [128, CC, C], F32))
                wp = bctx.enter_context(nc.sbuf_tensor("wp", [128, CC, C], F32))
                Qf = bctx.enter_context(nc.sbuf_tensor("Qf", [128, CC, NT], F32))
                Af = bctx.enter_context(nc.sbuf_tensor("Af", [128, CC, NT], F32))
                sA2 = Af[:, 6, :]
                sB2 = Af[:, 7, :]
                nrm = bctx.enter_context(nc.sbuf_tensor("nrm", [16, NT], F32))
                inv = bctx.enter_context(nc.sbuf_tensor("inv", [16, NT], F32))
                pq = [bctx.enter_context(nc.psum_tensor(f"pqb{i}", [128, NT], F32))
                      for i in range(2)]
                pnorm = bctx.enter_context(nc.psum_tensor("pnb", [16, NT], F32))

                def _load_b(sync):
                    S.wait(sync, "PE", ck[("A_kv", NCH - 1)])
                    S.inc(sync.dma_start(out=wq[:, :, 0:512],
                                         in_=wqkvT_r[:, :, 0:512]), "LKV", 16)
                    S.inc(sync.dma_start(out=wq[:, :, 512:C],
                                         in_=wqkvT_r[:, :, 512:C]), "LX", 16)
                    S.inc(sync.dma_start(out=wp[:], in_=wprojT_r), "LD", 16)
                block.sync(_load_b)
                LD_B = S.val("LD")
                LKV_B = S.val("LKV")
                LX_B = S.val("LX")

                def _kv_out(sync):
                    S.wait(sync, "DV", ck["A_fin"])
                    S.inc(sync.dma_start(out=kvar_i[:, 0:1024],
                                         in_=kvd[:].rearrange("p h m -> p (h m)")),
                          "CX", 16)
                    S.inc(sync.dma_start(out=kvar_i[:, 1024:1152],
                                         in_=KS[:].rearrange("p h m -> p (h m)")),
                          "CX", 16)
                block.sync(_kv_out)

                def _kv_cc(gp):
                    S.wait(gp, "CX", 32)
                    mi = gp.collective_compute(
                        "AllReduce", ALU.add,
                        replica_groups=[[0, 1, 2, 3], [4, 5, 6, 7]],
                        ins=[kvar_i.opt()], outs=[kvar_o.opt()])
                    ck["CC_kv"] = S.inc(mi, "CCOL")
                    S.wait(gp, "CCOL", ck["CC_kv"])
                    S.inc(gp.dma_start(out=kvd[:].rearrange("p h m -> p (h m)"),
                                       in_=kvar_o[:, 0:1024]), "CXB", 16)
                    S.inc(gp.dma_start(out=KS[:].rearrange("p h m -> p (h m)"),
                                       in_=kvar_o[:, 1024:1152]), "CXB", 16)
                block.gpsimd(_kv_cc)


                def _lnbx(tile, mid=None):
                    tsl = slice(tile * NT, (tile + 1) * NT)
                    _emit_ln(nc, block, S, ck, ("B", tile), mid=mid,
                             y_ap=lambda oc, tsl=tsl: X[:, oc, tsl],
                             sq_ap=lambda oc: Af[:, oc, :],
                             dst_ap=lambda oc, tsl=tsl: X[:, oc, tsl],
                             dep_dv=lambda oc, tile=tile: ck[("B_y1", tile, oc)],
                             dep_pe_sq=ck[("B_p", tile, CC - 1)],
                             onesc=onesc, ones1=ones1,
                             R0=R0, R1=R1, R2=R2, R3=R3, R4=R4, sq_rotate=False)

                    otmB = Af[:].rearrange("p a b -> p (a b)")
                    with contextlib.ExitStack() as pbx:
                        ptxB = [pbx.enter_context(
                            nc.psum_tensor(f"ptxB{tile}_{i}", [128, 128], F32))
                            for i in range(2)]
                        for j in range(NJ):
                            for oc in range(CC):
                                k = j * CC + oc

                                def _petxB(t, tile=tile, j=j, oc=oc, k=k):
                                    if k == 0:
                                        S.wait(t, "DV", ck[("LN", "B", tile, "app2")])
                                    if k >= 2:
                                        S.wait(t, "DV", ck[("BX_ev", tile, k - 2)])
                                    mi = t.transpose(
                                        ptxB[k % 2][:],
                                        X[:, oc,
                                          tile * NT + j * 128:tile * NT + (j + 1) * 128],
                                        ident[:])
                                    ck[("BX_tr", tile, k)] = S.inc(mi, "PE")
                                block.tensor(_petxB)

                                def _dvtxB(v, tile=tile, j=j, oc=oc, k=k):
                                    S.wait(v, "PE", ck[("BX_tr", tile, k)])
                                    prev = ck.get(("BX_st", tile, j - 1)) if j >= 1 else (
                                        ck.get(("BX_st", tile - 1, NJ - 1))
                                        if tile > 0 else 0)
                                    if prev:
                                        S.wait(v, "SH", prev)
                                    mi = v.tensor_copy(
                                        otmB[:, oc * 128:(oc + 1) * 128],
                                        ptxB[k % 2][:])
                                    ck[("BX_ev", tile, k)] = S.inc(mi, "DV")
                                block.vector(_dvtxB)

                            def _stB(sync, tile=tile, j=j):
                                S.wait(sync, "DV", ck[("BX_ev", tile, j * CC + CC - 1)])
                                r0 = tile * NT + j * 128
                                mi = sync.dma_start(out=x1h[r0:r0 + 128, 0:C],
                                                    in_=otmB[:, 0:C])
                                ck[("BX_st", tile, j)] = S.inc(mi, "SH", 16)
                            block.sync(_stB)

                for tile in range(NTILE):
                    tsl = slice(tile * NT, (tile + 1) * NT)

                    def _mid_ps1(tile=tile, tsl=tsl):
                        with contextlib.ExitStack() as ps1:
                            for ec in range(CC):
                                def _peq(t, ec=ec, tile=tile):
                                    if ec == 0 and tile == 0:
                                        S.wait(t, "LKV", LKV_B)
                                    if ec == 4 and tile == 0:
                                        S.wait(t, "LX", LX_B)
                                    if tile > 0 and ec == 0:
                                        S.wait(t, "DV",
                                               ck[("B_phi", tile - 1, CC - 2)])
                                    if tile > 0 and ec == 1:
                                        S.wait(t, "DV",
                                               ck[("B_phi", tile - 1, CC - 1)])
                                    if ec >= 2:
                                        S.wait(t, "DV", ck[("B_phi", tile, ec - 2)])
                                    for cc in range(CC):
                                        mi = t.matmul(pq[ec % 2][:],
                                                      wq[:, cc, ec * 128:(ec + 1) * 128],
                                                      X[:, cc, tsl],
                                                      start=(cc == 0), stop=(cc == CC - 1))
                                    ck[("B_q", tile, ec)] = S.inc(mi, "PE")
                                block.tensor(_peq)

                                def _dvq1(v, ec=ec, tile=tile):
                                    S.wait(v, "PE", ck[("B_q", tile, ec)])
                                    prev = ck.get(("B_exp", tile, ec - 1)) if ec > 0 else \
                                        (ck.get(("B_exp", tile - 1, CC - 1)) if tile > 0 else 0)
                                    if prev:
                                        S.wait(v, "AC", prev)
                                    mi = v.tensor_scalar(sA2, pq[ec % 2][:], 0.0, None,
                                                         ALU.min)
                                    ck[("B_min", tile, ec)] = S.inc(mi, "DV")
                                block.vector(_dvq1)

                                def _acq(a, ec=ec, tile=tile):
                                    S.wait(a, "DV", ck[("B_min", tile, ec)])
                                    prev = ck.get(("B_phi", tile, ec - 1)) if ec > 0 else \
                                        (ck.get(("B_phi", tile - 1, CC - 1)) if tile > 0 else 0)
                                    if prev:
                                        S.wait(a, "DV", prev)
                                    mi = a.activation(sB2, sA2, AF.Exp)
                                    ck[("B_exp", tile, ec)] = S.inc(mi, "AC")
                                block.scalar(_acq)

                                def _dvq2(v, ec=ec, tile=tile):
                                    S.wait(v, "AC", ck[("B_exp", tile, ec)])
                                    mi = v.scalar_tensor_tensor(Qf[:, ec, :], pq[ec % 2][:],
                                                                0.0, sB2, ALU.max, ALU.add)
                                    ck[("B_phi", tile, ec)] = S.inc(mi, "DV")
                                block.vector(_dvq2)

                    if tile > 0:
                        _lnbx(tile - 1, mid=_mid_ps1)
                    else:
                        _mid_ps1()

                    with contextlib.ExitStack() as ps2:
                        def _pen(t, tile=tile):
                            if tile == 0:
                                S.wait(t, "CXB", 32)
                            S.wait(t, "DV", ck[("B_phi", tile, CC - 1)])
                            for ec in range(CC):
                                mi = t.matmul(pnorm[:], KS[:, ec, :], Qf[:, ec, :],
                                              start=(ec == 0), stop=(ec == CC - 1))
                            ck[("B_nrm", tile)] = S.inc(mi, "PE")
                        block.tensor(_pen)

                        def _dvn(v, tile=tile):
                            S.wait(v, "PE", ck[("B_nrm", tile)])
                            c = S.inc(v.tensor_scalar(nrm[:], pnorm[:], EPS_ATTN, None,
                                                      ALU.add), "DV")
                            S.wait(v, "DV", c)
                            mi = v.reciprocal(inv[:], nrm[:])
                            ck[("B_inv", tile)] = S.inc(mi, "DV")
                        block.vector(_dvn)

                    with contextlib.ExitStack() as ps3:
                        pbc = [ps3.enter_context(
                            nc.psum_tensor(f"pbc{tile}_{i}", [128, NT], F32))
                            for i in range(2)]
                        po = [ps3.enter_context(
                            nc.psum_tensor(f"po{tile}_{i}", [128, NT], F32))
                            for i in range(2)]
                        invb = [sA2, sB2]
                        for ec in range(CC):
                            def _peo(t, ec=ec, tile=tile):
                                if tile > 0 and ec == 0:
                                    S.wait(t, "SH", ck[("BX_st", tile - 1, NJ - 1)])
                                if ec == 0:
                                    S.wait(t, "DV", ck[("B_inv", tile)])
                                if ec >= 2:
                                    S.wait(t, "DV", ck[("B_att", tile, ec - 2)])
                                t.matmul(pbc[ec % 2][:], bc16[:, ec, :], inv[:],
                                         start=True, stop=True)
                                mi = t.matmul(po[ec % 2][:], kvd[:, ec, :],
                                              Qf[:, ec, :], start=True, stop=True)
                                ck[("B_o", tile, ec)] = S.inc(mi, "PE")
                            block.tensor(_peo)

                            def _aci(a, ec=ec, tile=tile):
                                S.wait(a, "PE", ck[("B_o", tile, ec)])
                                if ec >= 2:
                                    S.wait(a, "DV", ck[("B_att", tile, ec - 2)])
                                mi = a.copy(invb[ec % 2][:], pbc[ec % 2][:])
                                ck[("B_ic", tile, ec)] = S.inc(mi, "AC")
                            block.scalar(_aci)

                            def _dvo(v, ec=ec, tile=tile):
                                S.wait(v, "PE", ck[("B_o", tile, ec)])
                                S.wait(v, "AC", ck[("B_ic", tile, ec)])
                                mi = v.tensor_tensor(Af[:, ec, :], po[ec % 2][:],
                                                     invb[ec % 2][:], ALU.mult)
                                ck[("B_att", tile, ec)] = S.inc(mi, "DV")
                            block.vector(_dvo)

                    with contextlib.ExitStack() as ps4:
                        pp = [ps4.enter_context(
                            nc.psum_tensor(f"pp{tile}_{i}", [128, NT], F32))
                            for i in range(2)]
                        for oc in range(CC):
                            def _pep(t, oc=oc, tile=tile):
                                if tile == 0 and oc == 0:
                                    S.wait(t, "GP", GP_AB)
                                    S.wait(t, "GP", GP_INIT)
                                    S.wait(t, "LD", LD_B)
                                if oc == 0:
                                    S.wait(t, "DV", ck[("B_att", tile, CC - 1)])
                                if oc >= 2:
                                    S.wait(t, "DV", ck[("B_y1", tile, oc - 2)])
                                t.matmul(pp[oc % 2][:],
                                         bproj_sb[0:1, oc * 128:(oc + 1) * 128],
                                         onesr[:], start=True, stop=False)
                                for cc in range(CC):
                                    mi = t.matmul(pp[oc % 2][:],
                                                  wp[:, cc, oc * 128:(oc + 1) * 128],
                                                  Af[:, cc, :],
                                                  start=False, stop=(cc == CC - 1))
                                ck[("B_p", tile, oc)] = S.inc(mi, "PE")
                            block.tensor(_pep)

                            def _dvp(v, oc=oc, tile=tile):
                                S.wait(v, "PE", ck[("B_p", tile, oc)])
                                mi = v.tensor_tensor(X[:, oc, tsl], pp[oc % 2][:],
                                                     X[:, oc, tsl], ALU.add)
                                ck[("B_y1", tile, oc)] = S.inc(mi, "DV")
                            block.vector(_dvp)

                _lnbx(NTILE - 1)

        with contextlib.ExitStack() as gctx:
            ge = gctx.enter_context(nc.sbuf_tensor("ge", [128, 16, ER], F32))
            gwt = gctx.enter_context(nc.sbuf_tensor("gwt", [128, 16, ER], F32))
            gmask = gctx.enter_context(nc.sbuf_tensor("gmask", [128, 16, ER], F32))
            gs = gctx.enter_context(nc.sbuf_tensor("gs", [128, 16, 1], F32))
            gr = gctx.enter_context(nc.sbuf_tensor("gr", [128, 16, 1], F32))
            gm8 = gctx.enter_context(nc.sbuf_tensor("gm8", [128, 16, 8], F32))
            gated_tm = gctx.enter_context(nc.sbuf_tensor("gated_tm", [128, 16, 16], F32))
            msk = gctx.enter_context(nc.sbuf_tensor("msk", [128, 16], F32))
            s1t = gctx.enter_context(nc.sbuf_tensor("s1t", [128, 16], F32))
            sm1 = gctx.enter_context(nc.sbuf_tensor("sm1", [128, 16], F32))
            sm2 = gctx.enter_context(nc.sbuf_tensor("sm2", [128, 16], F32))
            cs_sb = gctx.enter_context(nc.sbuf_tensor("cs_sb", [1, 16], F32))
            cs_col = gctx.enter_context(nc.sbuf_tensor("cs_col", [16, 1], F32))
            ioff = gctx.enter_context(nc.sbuf_tensor("ioff", [16, 1], F32))
            offs_row = gctx.enter_context(nc.sbuf_tensor("offs_row", [1, 16], F32))
            pgt = [gctx.enter_context(nc.psum_tensor(f"pgt{i}", [128, ER], F32))
                   for i in range(2)]
            ptg = [gctx.enter_context(nc.psum_tensor(f"ptg{i}", [16, 128], F32))
                   for i in range(2)]
            pslot = gctx.enter_context(nc.psum_tensor("pslot", [128, 16], F32))
            pcs = gctx.enter_context(nc.psum_tensor("pcs", [1, 16], F32))
            paux = gctx.enter_context(nc.psum_tensor("paux", [16, 16], F32))

            for ch in range(16):
                def _peg(t, ch=ch):
                    if ch == 0:
                        S.wait(t, "DV", ck[("BX_ev", NTILE - 1, NJ * CC - 1)])
                    if ch >= 2:
                        S.wait(t, "AC", ck[("G_exp", ch - 2)])
                    for cc in range(CC):
                        t.matmul(pgt[ch % 2][:], X[:, cc, ch * 128:(ch + 1) * 128],
                                 gw_sb[:, cc, :], start=(cc == 0), stop=False)
                    mi = t.matmul(pgt[ch % 2][:], ones1[:], gb_sb[:],
                                  start=False, stop=True)
                    ck[("G_mm", ch)] = S.inc(mi, "PE")
                block.tensor(_peg)

                def _acg(a, ch=ch):
                    S.wait(a, "PE", ck[("G_mm", ch)])
                    mi = a.activation(ge[:, ch, :], pgt[ch % 2][:], AF.Exp)
                    ck[("G_exp", ch)] = S.inc(mi, "AC")
                block.scalar(_acg)

            def _dvg(v):
                S.wait(v, "AC", ck[("G_exp", 15)])
                v.memset(gated_tm[:, :, 15:16], 1.0)
                c = S.inc(v.tensor_reduce(gs[:], ge[:], AX.X, ALU.add), "DV")
                S.wait(v, "DV", c)
                c = S.inc(v.reciprocal(gr[:], gs[:]), "DV")
                S.wait(v, "DV", c)
                c = S.inc(v.tensor_tensor(gwt[:], ge[:],
                                          gr[:].to_broadcast([128, 16, ER]),
                                          ALU.mult), "DV")
                S.wait(v, "DV", c)
                for ch in range(16):
                    mi = v.max(out=gm8[:, ch, :], in_=gwt[:, ch, :])
                c = S.inc(mi, "DV")
                S.wait(v, "DV", c)
                c = S.inc(v.tensor_tensor(gmask[:], gwt[:],
                                          gm8[:, :, 2:3].to_broadcast([128, 16, ER]),
                                          ALU.is_ge), "DV")
                S.wait(v, "DV", c)
                mi = v.tensor_tensor(gated_tm[:, :, 0:ER], gwt[:], gmask[:], ALU.mult)
                ck["G_tm"] = S.inc(mi, "DV")
            block.vector(_dvg)

            for c16 in range(16):
                def _stg(sync, c16=c16):
                    S.wait(sync, "DV", ck["G_tm"])
                    mi = sync.dma_start(
                        out=x1h[c16 * 128:(c16 + 1) * 128, C:C + 16],
                        in_=gated_tm[:, c16, :])
                    ck[("G_st", c16)] = S.inc(mi, "SH", 16)
                block.sync(_stg)

            for ch in range(16):
                def _petr(t, ch=ch):
                    S.wait(t, "DV", ck["G_tm"])
                    if ch >= 2:
                        S.wait(t, "DV", ck[("G_fm", ch - 2)])
                    mi = t.transpose(ptg[ch % 2][:], gated_tm[:, ch, :], ident[:])
                    ck[("G_tr", ch)] = S.inc(mi, "PE")
                block.tensor(_petr)

                def _dvtr(v, ch=ch):
                    S.wait(v, "PE", ck[("G_tr", ch)])
                    mi = v.tensor_copy(gated_fm[:, ch * 128:(ch + 1) * 128],
                                       ptg[ch % 2][:])
                    ck[("G_fm", ch)] = S.inc(mi, "DV")
                block.vector(_dvtr)

            for e in range(ER):
                def _dvm(v, e=e):
                    S.wait(v, "DV", ck["G_tm"])
                    if e > 0:
                        S.wait(v, "DV", ck[("SL", e - 1)])
                    mi = v.tensor_scalar(msk[:], gated_tm[:, :, e], 0.0, None,
                                         ALU.is_gt)
                    ck[("SLm", e)] = S.inc(mi, "DV")
                block.vector(_dvm)

                def _pes1(t, e=e):
                    S.wait(t, "DV", ck[("SLm", e)])
                    t.matmul(pslot[:], ltri_sb[:], msk[:], start=True, stop=False)
                    mi = t.matmul(pcs[:], onesc[:], msk[:], start=True, stop=True)
                    ck[("SLa", e)] = S.inc(mi, "PE")
                block.tensor(_pes1)

                def _dvc1(v, e=e):
                    S.wait(v, "PE", ck[("SLa", e)])
                    mi = v.tensor_copy(cs_sb[:], pcs[:])
                    ck[("SLb", e)] = S.inc(mi, "DV")
                block.vector(_dvc1)

                def _pet1(t, e=e):
                    S.wait(t, "DV", ck[("SLb", e)])
                    mi = t.transpose(paux[:, 0:1], cs_sb[:], ident[0:1, 0:1])
                    ck[("SLc", e)] = S.inc(mi, "PE")
                block.tensor(_pet1)

                def _dvc2(v, e=e):
                    S.wait(v, "PE", ck[("SLc", e)])
                    mi = v.tensor_copy(cs_col[:], paux[:, 0:1])
                    ck[("SLd", e)] = S.inc(mi, "DV")
                block.vector(_dvc2)

                def _pet2(t, e=e):
                    S.wait(t, "DV", ck[("SLd", e)])
                    mi = t.matmul(paux[:, 1:2], ltri_sb[0:16, 0:16], cs_col[:],
                                  start=False, stop=False, skip_group_check=True)
                    ck[("SLe", e)] = S.inc(mi, "PE")
                block.tensor(_pet2)

                def _dvc3(v, e=e):
                    S.wait(v, "PE", ck[("SLe", e)])
                    mi = v.tensor_copy(ioff[:], paux[:, 1:2])
                    ck[("SLf", e)] = S.inc(mi, "DV")
                block.vector(_dvc3)

                def _pet3(t, e=e):
                    S.wait(t, "DV", ck[("SLf", e)])
                    mi = t.transpose(pcs[:], ioff[:], ident[0:16, 0:16])
                    ck[("SLg", e)] = S.inc(mi, "PE")
                block.tensor(_pet3)

                def _dvc4(v, e=e):
                    S.wait(v, "PE", ck[("SLg", e)])
                    mi = v.tensor_copy(offs_row[:], pcs[:])
                    ck[("SLh", e)] = S.inc(mi, "DV")
                block.vector(_dvc4)

                def _pet4(t, e=e):
                    S.wait(t, "DV", ck[("SLh", e)])
                    mi = t.matmul(pslot[:], ones1[:], offs_row[:],
                                  start=False, stop=True)
                    ck[("SLi", e)] = S.inc(mi, "PE")
                block.tensor(_pet4)

                def _dvc5(v, e=e):
                    S.wait(v, "PE", ck[("SLi", e)])
                    c = S.inc(v.tensor_scalar(sm1[:], pslot[:], 127.5, None,
                                              ALU.is_gt), "DV")
                    S.wait(v, "DV", c)
                    for kth in range(2, NCAP):
                        c = S.inc(v.tensor_scalar(sm2[:], pslot[:],
                                                  kth * 128 - 0.5, None,
                                                  ALU.is_gt), "DV")
                        S.wait(v, "DV", c)
                        c = S.inc(v.tensor_tensor(sm1[:], sm1[:], sm2[:], ALU.add),
                                  "DV")
                        S.wait(v, "DV", c)
                    c = S.inc(v.tensor_scalar(sm2[:], sm1[:], float(NCAP * 128 - 1),
                                              None, ALU.mult), "DV")
                    S.wait(v, "DV", c)
                    c = S.inc(v.scalar_tensor_tensor(s1t[:], pslot[:], float(NCAP),
                                                     sm2[:], ALU.mult, ALU.subtract),
                              "DV")
                    S.wait(v, "DV", c)
                    c = S.inc(v.tensor_scalar(sm2[:], pslot[:], CAPE - 0.5, None,
                                              ALU.is_gt), "DV")
                    S.wait(v, "DV", c)
                    c = S.inc(v.scalar_tensor_tensor(s1t[:], sm2[:], BIGF, s1t[:],
                                                     ALU.mult, ALU.add), "DV")
                    S.wait(v, "DV", c)
                    c = S.inc(v.scalar_tensor_tensor(sm2[:], msk[:], -BIGF, s1t[:],
                                                     ALU.mult, ALU.add), "DV")
                    S.wait(v, "DV", c)
                    c = S.inc(v.tensor_scalar(s1t[:], sm2[:], BIGF, None, ALU.add),
                              "DV")
                    S.wait(v, "DV", c)
                    mi = v.tensor_copy(slotsb[:, e, :], s1t[:])
                    ck[("SL", e)] = S.inc(mi, "DV")
                block.vector(_dvc5)
            ck["slots_done"] = ck[("SL", ER - 1)]

        with contextlib.ExitStack() as cctx:
            yac = cctx.enter_context(nc.sbuf_tensor("yac", [128, CC, TOWN], F32))
            he = cctx.enter_context(nc.sbuf_tensor("he", [128, NFC, NT], F32))
            w1r = cctx.enter_context(nc.sbuf_tensor("w1r", [128, NRING1, CC, 128], F32))
            w2r = cctx.enter_context(nc.sbuf_tensor("w2r", [128, NRING2, C], F32))
            otmC = cctx.enter_context(nc.sbuf_tensor("otmC", [128, 1, C], F32))

            g1_idx = [0]
            g2_idx = [0]
            first_c = [True]
            w1_last = {}
            w2_last = {}

            def _issue_w1(sync, src, sw):
                gi = g1_idx[0]
                if sw:
                    S.wait(sync, "DV", ck[("BX_ev", NTILE - 1, NJ * CC - 1)])
                    S.wait(sync, "SH", S.val("SH"))
                if gi >= NRING1:
                    S.wait(sync, "PE", w1_last[gi % NRING1])
                S.inc(sync.dma_start(out=w1r[:, gi % NRING1], in_=src),
                      f"LW1_{gi % NRING1}", 16)
                g1_idx[0] += 1

            def _issue_w2(gp, src, sw):
                gi = g2_idx[0]
                if sw:
                    S.wait(gp, "DV", ck[("BX_ev", NTILE - 1, NJ * CC - 1)])
                    S.wait(gp, "SH", S.val("SH"))
                if gi >= NRING2:
                    S.wait(gp, "PE", w2_last[gi % NRING2])
                S.inc(gp.dma_start(out=w2r[:, gi % NRING2], in_=src),
                      f"LW2_{gi % NRING2}", 16)
                g2_idx[0] += 1

            for tile in range(NTILE):
                tsl = slice(tile * NT, (tile + 1) * NT)
                for ei in range(NEXP):
                    idx = ei

                    with contextlib.ExitStack() as p1s:
                        ph = [p1s.enter_context(
                            nc.psum_tensor(f"ph{tile}_{ei}_{i}", [128, NT], F32))
                            for i in range(2)]

                        gexp0_w2 = [None]
                        for fc in range(NFC):
                            sw = first_c[0]
                            first_c[0] = False

                            def _i1(sync, idx=idx, fc=fc, sw=sw):
                                _issue_w1(sync, sw1_d[idx * NFC + fc], sw)
                            block.sync(_i1)

                            if fc == 0:
                                gexp0_w2[0] = g2_idx[0]

                                def _i2a(gp, idx=idx, sw=sw):
                                    _issue_w2(gp, sw2_d[idx * NFC + 0], sw)
                                    _issue_w2(gp, sw2_d[idx * NFC + 1], sw)
                                block.gpsimd(_i2a)

                            gi1 = g1_idx[0] - 1

                            def _pe1(t, tile=tile, ei=ei, fc=fc, gi1=gi1):
                                S.wait(t, f"LW1_{gi1 % NRING1}",
                                       (gi1 // NRING1 + 1) * 16)
                                if fc == 0:
                                    if tile == 0 and ei == 0:
                                        S.wait(t, "DV", ck[("G_fm", 15)])
                                        S.wait(t, "AC", ck[("G_exp", 15)])
                                        S.wait(t, "DV", ck["slots_done"])
                                    elif ei == 0:
                                        S.wait(t, "DV",
                                               ck[("D_ev", tile - 1, NJ * CC - 1)])
                                    else:
                                        S.wait(t, "DV", ck[("C_yev", tile, ei - 1)])
                                if fc >= 2:
                                    S.wait(t, "AC", ck[("C_h", tile, ei, fc - 2)])
                                for cc in range(CC):
                                    mi = t.matmul(ph[fc % 2][:],
                                                  w1r[:, gi1 % NRING1, cc, :],
                                                  X[:, cc, tsl],
                                                  start=(cc == 0), stop=(cc == CC - 1))
                                ck[("C_w1", tile, ei, fc)] = S.inc(mi, "PE")
                            block.tensor(_pe1)
                            w1_last[gi1 % NRING1] = ck[("C_w1", tile, ei, fc)]

                            def _ac1(a, tile=tile, ei=ei, idx=idx, fc=fc):
                                S.wait(a, "PE", ck[("C_w1", tile, ei, fc)])
                                bias = b1s_sb[:, idx * NFC + fc:idx * NFC + fc + 1]
                                mi = a.activation(he[:, fc, :], ph[fc % 2][:], AF.Gelu,
                                                  bias=bias)
                                ck[("C_h", tile, ei, fc)] = S.inc(mi, "AC")
                            block.scalar(_ac1)

                    with contextlib.ExitStack() as p2s:
                        py = [p2s.enter_context(
                            nc.psum_tensor(f"py{tile}_{ei}_{o}", [128, NT], F32))
                            for o in range(CC)]
                        gexp0 = gexp0_w2[0]

                        for fc in range(NFC):
                            gi2 = gexp0 + fc

                            def _pe2(t, tile=tile, ei=ei, fc=fc, gi2=gi2):
                                S.wait(t, f"LW2_{gi2 % NRING2}",
                                       (gi2 // NRING2 + 1) * 16)
                                if fc == 0:
                                    S.wait(t, "AC", ck[("C_h", tile, ei, NFC - 1)])
                                    if ei == 0:
                                        S.wait(t, "DV", ck[("G_fm", 15)])
                                        for oc in range(CC):
                                            t.matmul(py[oc][:],
                                                     b2_sb[:, oc * 128:(oc + 1) * 128],
                                                     gated_fm[:, tsl],
                                                     start=True, stop=False)
                                for oc in range(CC):
                                    mi = t.matmul(py[oc][:],
                                                  w2r[:, gi2 % NRING2,
                                                      oc * 128:(oc + 1) * 128],
                                                  he[:, fc, :],
                                                  start=(fc == 0 and ei != 0),
                                                  stop=(fc == NFC - 1))
                                ck[("C_w2f", tile, ei, fc)] = S.inc(mi, "PE")
                            block.tensor(_pe2)
                            w2_last[gi2 % NRING2] = ck[("C_w2f", tile, ei, fc)]

                            if fc + 2 < NFC:
                                def _i2b(gp, idx=idx, fc=fc):
                                    _issue_w2(gp, sw2_d[idx * NFC + fc + 2], False)
                                block.gpsimd(_i2b)

                        def _dvy(v, tile=tile, ei=ei):
                            S.wait(v, "PE", ck[("C_w2f", tile, ei, NFC - 1)])
                            if ei > 0:
                                S.wait(v, "DV", ck[("C_yev", tile, ei - 1)])
                            for oc in range(CC):
                                if ei == 0:
                                    mi = v.scalar_tensor_tensor(
                                        yac[:, oc, tsl], X[:, oc, tsl], 2.0,
                                        py[oc][:], ALU.mult, ALU.add)
                                else:
                                    mi = v.tensor_tensor(yac[:, oc, tsl],
                                                         yac[:, oc, tsl],
                                                         py[oc][:], ALU.add)
                            ck[("C_yev", tile, ei)] = S.inc(mi, "DV")
                        block.vector(_dvy)

                with contextlib.ExitStack() as pds:
                    ptx = [pds.enter_context(
                        nc.psum_tensor(f"ptx{tile}_{i}", [128, 128], F32))
                        for i in range(2)]
                    for j in range(NJ):
                        for oc in range(CC):
                            k = j * CC + oc

                            def _petx(t, tile=tile, j=j, oc=oc, k=k):
                                if k == 0:
                                    S.wait(t, "DV", ck[("C_yev", tile, NEXP - 1)])
                                if k >= 2:
                                    S.wait(t, "DV", ck[("D_ev", tile, k - 2)])
                                mi = t.transpose(
                                    ptx[k % 2][:],
                                    yac[:, oc,
                                        tile * NT + j * 128:tile * NT + (j + 1) * 128],
                                    ident[:])
                                ck[("D_tr", tile, k)] = S.inc(mi, "PE")
                            block.tensor(_petx)

                            def _dvtx(v, tile=tile, j=j, oc=oc, k=k):
                                S.wait(v, "PE", ck[("D_tr", tile, k)])
                                prev = ck.get(("D_st", tile, j - 1)) if j >= 1 else \
                                    (ck.get(("D_st", tile - 1, NJ - 1))
                                     if tile > 0 else 0)
                                if prev:
                                    S.wait(v, "SH", prev)
                                mi = v.tensor_copy(
                                    otmC[:, 0, oc * 128:(oc + 1) * 128],
                                    ptx[k % 2][:])
                                ck[("D_ev", tile, k)] = S.inc(mi, "DV")
                            block.vector(_dvtx)

                        def _st(sync, tile=tile, j=j):
                            S.wait(sync, "DV", ck[("D_ev", tile, j * CC + CC - 1)])
                            r0 = tile * NT + j * 128
                            mi = sync.dma_start(out=ybuf[r0:r0 + 128, :],
                                                in_=otmC[:, 0, :])
                            ck[("D_st", tile, j)] = S.inc(mi, "SH", 16)
                        block.sync(_st)
        SH_DENSE = S.val("SH")

        mctx.close()

        with contextlib.ExitStack() as rctx:
            g_tm = [rctx.enter_context(nc.sbuf_tensor(f"g_tm{i}", [128, NCAP, ROWW], F32))
                    for i in range(2)]
            idsb = rctx.enter_context(nc.sbuf_tensor("idsb", [128, ER, NCAP],
                                                     mybir.dt.uint32))
            Xe = [rctx.enter_context(nc.sbuf_tensor(f"Xe{i}", [128, CC, CAP], F32))
                  for i in range(2)]
            grow = [rctx.enter_context(nc.sbuf_tensor(f"grow{i}", [1, CAP], F32))
                    for i in range(2)]
            G_sb = [rctx.enter_context(nc.sbuf_tensor(f"G_sb{i}", [128, CAP], F32))
                    for i in range(2)]
            her = rctx.enter_context(nc.sbuf_tensor("her", [128, NFC, CAP], F32))
            w1b = rctx.enter_context(nc.sbuf_tensor("w1b", [128, NRING1, CC, 128], F32))
            w2b = rctx.enter_context(nc.sbuf_tensor("w2b", [128, NRING2, C], F32))
            idinit_row = rctx.enter_context(nc.sbuf_tensor("idinit_row", [1, CAP],
                                                           mybir.dt.uint32))

            def _ld_idinit(sync):
                S.wait(sync, "DV", ck[("D_ev", NTILE - 1, NJ * CC - 1)])
                S.wait(sync, "SH", SH_DENSE)
                S.inc(sync.dma_start(out=idinit_row[:], in_=idinit_d), "LD", 16)
            block.sync(_ld_idinit)
            LD_IDINIT = S.val("LD")

            r1_idx = [g1_idx[0]]
            r2_idx = [g2_idx[0]]
            r1_last = dict(w1_last)
            r2_last = dict(w2_last)

            def _issue_rw1(sync, src, first):
                gi = r1_idx[0]
                if first:
                    S.wait(sync, "DV", ck[("D_ev", NTILE - 1, NJ * CC - 1)])
                    S.wait(sync, "SH", SH_DENSE)
                if gi >= NRING1:
                    S.wait(sync, "PE", r1_last[gi % NRING1])
                S.inc(sync.dma_start(out=w1b[:, gi % NRING1], in_=src),
                      f"LW1_{gi % NRING1}", 16)
                r1_idx[0] += 1

            def _issue_rw2(gp, src, first):
                gi = r2_idx[0]
                if first:
                    S.wait(gp, "DV", ck[("D_ev", NTILE - 1, NJ * CC - 1)])
                    S.wait(gp, "SH", SH_DENSE)
                if gi >= NRING2:
                    S.wait(gp, "PE", r2_last[gi % NRING2])
                S.inc(gp.dma_start(out=w2b[:, gi % NRING2], in_=src),
                      f"LW2_{gi % NRING2}", 16)
                r2_idx[0] += 1

            regs = {}

            def _gp_regs(gp):
                rb2047 = gp.alloc_register()
                gp.reg_mov(rb2047, TOWN - 1)
                rb767 = gp.alloc_register()
                gp.reg_mov(rb767, CAP - 1)
                regs["b2047"] = rb2047
                regs["b767"] = rb767
            block.gpsimd(_gp_regs)

            def _gp_fetch(gp, e):
                par = e % 2
                if e >= 2:
                    S.wait(gp, "PE", ck[("R_trl", e - 2)])
                    S.wait(gp, "SC", ck[("R_sc", e - 2)])
                    S.wait(gp, "DV", ck[("R_otm", e - 2, NCAP * CC - 1)])
                if e == 0:
                    S.wait(gp, "SH", SH_DENSE)
                    S.wait(gp, "DV", ck[("D_ev", NTILE - 1, NJ * CC - 1)])
                    S.wait(gp, "DV", ck["slots_done"])
                    S.wait(gp, "LD", LD_IDINIT)
                mi = gp.dma_start(out=idh[e].rearrange("r o -> o (r)"),
                                  in_=idinit_row[:])
                c_init = S.inc(mi, "IGI", 16)
                S.wait(gp, "IGI", c_init)
                for c16 in range(16):
                    mi = gp.indirect_dma_start(
                        out=idh[e],
                        out_offset=bass.IndirectOffsetOnAxis(
                            slotsb[:, e, c16:c16 + 1], 0),
                        in_=tokid_sb[:, c16:c16 + 1], in_offset=None,
                        bounds_check=regs["b767"], oob_is_err=False)
                    c_sc = S.inc(mi, "IGS", 16)
                S.wait(gp, "IGS", c_sc)
                mi = gp.dma_start(out=idsb[:, e, :],
                                  in_=idh[e].rearrange("(p k) o -> p (k o)", p=128))
                ck[("R_relo", e)] = S.inc(mi, "IGR", 16)
                S.wait(gp, "IGR", ck[("R_relo", e)])
                gsem = f"IGG{par}"
                for k in range(NCAP):
                    mi = gp.indirect_dma_start(
                        out=g_tm[par][:, k, :], out_offset=None,
                        in_=x1h,
                        in_offset=bass.IndirectOffsetOnAxis(
                            idsb[:, e, k:k + 1], 0),
                        bounds_check=regs["b2047"], oob_is_err=False)
                    c_g = S.inc(mi, gsem, 16)
                ck[("R_gath", e)] = (gsem, c_g)

            def _emit_transposes(e, ptr2, pG):
                par = e % 2

                for k in range(NCAP):
                    for oc in range(CC):
                        kk = k * CC + oc

                        def _ptr(t, e=e, par=par, k=k, oc=oc, kk=kk):
                            if kk == 0:
                                S.wait(t, ck[("R_gath", e)][0], ck[("R_gath", e)][1])
                                if e >= 2:
                                    S.wait(t, "DV", ck[("R_xe", e - 2, NCAP * CC - 1)])
                            if kk >= 2:
                                S.wait(t, "DV", ck[("R_xe", e, kk - 2)])
                            mi = t.transpose(ptr2[kk % 2][:],
                                             g_tm[par][:, k, oc * 128:(oc + 1) * 128],
                                             ident[:])
                            ck[("R_xtr", e, kk)] = S.inc(mi, "PE")
                        block.tensor(_ptr)

                        def _dve(v, e=e, par=par, k=k, oc=oc, kk=kk):
                            S.wait(v, "PE", ck[("R_xtr", e, kk)])
                            mi = v.tensor_copy(Xe[par][:, oc, k * 128:(k + 1) * 128],
                                               ptr2[kk % 2][:])
                            ck[("R_xe", e, kk)] = S.inc(mi, "DV")
                        block.vector(_dve)

                for k in range(NCAP):
                    def _ptg2(t, e=e, par=par, k=k):
                        if k < 2:
                            S.wait(t, "DV", ck[("R_xe", e, NCAP * CC - 2 + k)])
                        if k >= 2:
                            S.wait(t, "DV", ck[("R_ge", e, k - 2)])
                        mi = t.transpose(ptr2[k % 2][0:1, :],
                                         g_tm[par][:, k, C + e:C + e + 1],
                                         ident[:])
                        ck[("R_gtr", e, k)] = S.inc(mi, "PE")
                    block.tensor(_ptg2)

                    def _dvg2(v, e=e, par=par, k=k):
                        S.wait(v, "PE", ck[("R_gtr", e, k)])
                        mi = v.tensor_copy(grow[par][0:1, k * 128:(k + 1) * 128],
                                           ptr2[k % 2][0:1, :])
                        ck[("R_ge", e, k)] = S.inc(mi, "DV")
                    block.vector(_dvg2)

                def _pgb(t, e=e, par=par):
                    S.wait(t, "DV", ck[("R_ge", e, NCAP - 1)])
                    for (s0, s1) in SUBS:
                        mi = t.matmul(pG[:, s0:s1], ones1[:],
                                      grow[par][0:1, s0:s1], start=True, stop=True)
                    ck[("R_pg", e)] = S.inc(mi, "PE")
                block.tensor(_pgb)

                def _acg2(a, e=e, par=par):
                    S.wait(a, "PE", ck[("R_pg", e)])
                    mi = a.copy(G_sb[par][:, 0:CAPE], pG[:, 0:CAPE])
                    ck[("R_gsb", e)] = S.inc(mi, "AC")
                block.scalar(_acg2)

                ck[("R_trl", e)] = ck[("R_pg", e)]

            def _gp_f0(gp):
                _gp_fetch(gp, 0)
            block.gpsimd(_gp_f0)
            with contextlib.ExitStack() as t0s:
                ptr2_0 = [t0s.enter_context(
                    nc.psum_tensor(f"rtr0_{i}", [128, 128], F32)) for i in range(2)]
                pG_0 = t0s.enter_context(nc.psum_tensor("rpg0", [128, CAP], F32))
                _emit_transposes(0, ptr2_0, pG_0)

            for e in range(ER):
                par = e % 2
                if e + 1 < ER:
                    def _gp_f(gp, e=e):
                        _gp_fetch(gp, e + 1)
                    block.gpsimd(_gp_f)

                with contextlib.ExitStack() as hs:
                    ph = [hs.enter_context(
                        nc.psum_tensor(f"rph{e}_{i}", [128, CAP], F32))
                        for i in range(2)]

                    for fc in range(NFC):
                        def _i1r(sync, e=e, fc=fc):
                            _issue_rw1(sync, rw1_d[e, fc], e == 0 and fc == 0)
                        block.sync(_i1r)
                        gi1 = r1_idx[0] - 1

                        def _rpe1(t, e=e, par=par, fc=fc, gi1=gi1):
                            S.wait(t, f"LW1_{gi1 % NRING1}", (gi1 // NRING1 + 1) * 16)
                            if fc == 0:
                                S.wait(t, "DV", ck[("R_xe", e, NCAP * CC - 1)])
                                S.wait(t, "DV", ck[("R_ge", e, NCAP - 1)])
                                S.wait(t, "AC", ck[("R_gsb", e)])
                                if e > 0:
                                    S.wait(t, "DV", ck[("R_yev", e - 1)])
                                    S.wait(t, "DV",
                                           ck[("R_otm", e - 1, NCAP * CC - 1)])
                            if fc >= 2:
                                S.wait(t, "AC", ck[("R_h", e, fc - 2)])
                            for (s0, s1) in SUBS:
                                for cc in range(CC):
                                    mi = t.matmul(ph[fc % 2][:, s0:s1],
                                                  w1b[:, gi1 % NRING1, cc, :],
                                                  Xe[par][:, cc, s0:s1],
                                                  start=(cc == 0), stop=(cc == CC - 1))
                            ck[("R_w1", e, fc)] = S.inc(mi, "PE")
                        block.tensor(_rpe1)
                        r1_last[gi1 % NRING1] = ck[("R_w1", e, fc)]

                        def _rac1(a, e=e, fc=fc):
                            S.wait(a, "PE", ck[("R_w1", e, fc)])
                            mi = a.activation(her[:, fc, 0:CAPE],
                                              ph[fc % 2][:, 0:CAPE], AF.Gelu,
                                              bias=b1r_sb[:, e, fc:fc + 1])
                            ck[("R_h", e, fc)] = S.inc(mi, "AC")
                        block.scalar(_rac1)

                    if e + 1 < ER:
                        with contextlib.ExitStack() as ts2:
                            ptr2_n = [ts2.enter_context(
                                nc.psum_tensor(f"rtr{e + 1}_{i}", [128, 128], F32))
                                for i in range(2)]
                            pG_n = ts2.enter_context(
                                nc.psum_tensor(f"rpg{e + 1}", [128, CAP], F32))
                            _emit_transposes(e + 1, ptr2_n, pG_n)

                with contextlib.ExitStack() as ys:
                    py = [ys.enter_context(
                        nc.psum_tensor(f"rpy{e}_{o}", [128, NT], F32))
                        for o in range(CC)]

                    for si, (s0, s1) in enumerate(SUBS):
                        sw_ = s1 - s0
                        for fc in range(NFC):
                            def _i2r(gp, e=e, fc=fc):
                                _issue_rw2(gp, rw2_d[e, fc],
                                           e == 0 and si == 0 and fc == 0)
                            block.gpsimd(_i2r)
                            gi2 = r2_idx[0] - 1

                            def _rpe2(t, e=e, par=par, si=si, s0=s0, s1=s1, sw_=sw_,
                                      fc=fc, gi2=gi2):
                                S.wait(t, f"LW2_{gi2 % NRING2}",
                                       (gi2 // NRING2 + 1) * 16)
                                if fc == 0 and si == 0:
                                    S.wait(t, "AC", ck[("R_h", e, NFC - 1)])
                                    S.wait(t, "AC", ck[("R_gsb", e)])
                                    if e + 1 < ER:
                                        S.wait(t, "AC", ck[("R_gsb", e + 1)])
                                        S.wait(t, "DV", ck[("R_ge", e + 1, NCAP - 1)])
                                        S.wait(t, "DV",
                                               ck[("R_xe", e + 1, NCAP * CC - 1)])
                                if fc == 0 and si == 1:
                                    S.wait(t, "DV", ck[("R_oev", e, 0)])
                                for oc in range(CC):
                                    mi = t.matmul(py[oc][:, 0:sw_],
                                                  w2b[:, gi2 % NRING2,
                                                      oc * 128:(oc + 1) * 128],
                                                  her[:, fc, s0:s1],
                                                  start=(fc == 0), stop=(fc == NFC - 1))
                                    if fc == NFC - 1:
                                        ck[("R_w2oc", e, si, oc)] = S.inc(mi, "PE")
                                if fc == NFC - 1:
                                    ck[("R_w2", e, si, fc)] = ck[("R_w2oc", e, si,
                                                                  CC - 1)]
                                else:
                                    ck[("R_w2", e, si, fc)] = S.inc(mi, "PE")
                            block.tensor(_rpe2)
                            r2_last[gi2 % NRING2] = ck[("R_w2", e, si, fc)]

                        def _rdvy(v, e=e, par=par, si=si, s0=s0, s1=s1, sw_=sw_):
                            for oc in range(CC):
                                S.wait(v, "PE", ck[("R_w2oc", e, si, oc)])
                                mi = v.scalar_tensor_tensor(
                                    Xe[par][:, oc, s0:s1], py[oc][:, 0:sw_], 1.0,
                                    G_sb[par][:, s0:s1], ALU.mult, ALU.mult)
                                ck[("R_oev_oc", e, si, oc)] = S.inc(mi, "DV")
                            ck[("R_oev", e, si)] = ck[("R_oev_oc", e, si, CC - 1)]
                        block.vector(_rdvy)
                    ck[("R_yev", e)] = ck[("R_oev", e, len(SUBS) - 1)]

                with contextlib.ExitStack() as os_:
                    ptxr = [os_.enter_context(
                        nc.psum_tensor(f"rptx{e}_{i}", [128, 128], F32))
                        for i in range(2)]
                    for k in range(NCAP):
                        for oc in range(CC):
                            kk = k * CC + oc

                            def _rptx(t, e=e, k=k, oc=oc, kk=kk):
                                if kk < CC:
                                    S.wait(t, "DV",
                                           ck[("R_oev_oc", e, len(SUBS) - 1, oc)])
                                if kk == 0:
                                    if e > 0:
                                        S.wait(t, "SC", ck[("R_sc", e - 1)])
                                if kk >= 2:
                                    S.wait(t, "DV", ck[("R_otm", e, kk - 2)])
                                mi = t.transpose(ptxr[kk % 2][:],
                                                 Xe[e % 2][:, oc, k * 128:(k + 1) * 128],
                                                 ident[:])
                                ck[("R_otr", e, kk)] = S.inc(mi, "PE")
                            block.tensor(_rptx)

                            def _rdvt(v, e=e, k=k, oc=oc, kk=kk):
                                S.wait(v, "PE", ck[("R_otr", e, kk)])
                                mi = v.tensor_copy(
                                    g_tm[e % 2][:, k, oc * 128:(oc + 1) * 128],
                                    ptxr[kk % 2][:])
                                ck[("R_otm", e, kk)] = S.inc(mi, "DV")
                            block.vector(_rdvt)

                def _gp_sc(gp, e=e, par=par):
                    S.wait(gp, "DV", ck[("R_otm", e, NCAP * CC - 1)])
                    if e == 2 and DBG_E2:
                        gp.dma_start(out=dbg1, in_=g_tm[par][:]).then_inc(
                            S.h["SC"], 16)
                        S.v["SC"] += 16
                        gp.dma_start(out=dbg2, in_=G_sb[par][:]).then_inc(
                            S.h["SC"], 16)
                        S.v["SC"] += 16
                        gp.wait_ge(S.h["SC"], S.v["SC"])
                    if e == 0:
                        S.wait(gp, "SH", SH_DENSE)
                    else:
                        S.wait(gp, "SC", ck[("R_sc", e - 1)])
                    for k in range(NCAP):
                        mi = gp.indirect_dma_start(
                            out=ybuf,
                            out_offset=bass.IndirectOffsetOnAxis(
                                idsb[:, e, k:k + 1], 0),
                            in_=g_tm[par][:, k, 0:C], in_offset=None,
                            bounds_check=regs["b2047"], oob_is_err=False,
                            compute_op=ALU.add)
                        c_s = S.inc(mi, "SC", 16)
                    S.wait(gp, "SC", c_s)
                    ck[("R_sc", e)] = c_s
                    if e == 2 and DBG_E2:
                        gp.dma_start(out=dbg3, in_=ybuf[0:128, :]).then_inc(
                            S.h["SC"], 16)
                        S.v["SC"] += 16
                        gp.wait_ge(S.h["SC"], S.v["SC"])
                        ck[("R_sc", e)] = S.v["SC"]
                block.gpsimd(_gp_sc)
        SC_ALL = S.val("SC")

        with contextlib.ExitStack() as lctx:
            ysb = lctx.enter_context(nc.sbuf_tensor("ysb", [128, 16, C], F32))
            sqs = lctx.enter_context(nc.sbuf_tensor("sqs", [128, 2, C], F32))
            mcolv = lctx.enter_context(nc.sbuf_tensor("mcolv", [128, 16], F32))
            scolv = lctx.enter_context(nc.sbuf_tensor("scolv", [128, 16], F32))
            vcolv = lctx.enter_context(nc.sbuf_tensor("vcolv", [128, 16], F32))
            sdcv = lctx.enter_context(nc.sbuf_tensor("sdcv", [128, 16], F32))
            rscv = lctx.enter_context(nc.sbuf_tensor("rscv", [128, 16], F32))
            bcolv = lctx.enter_context(nc.sbuf_tensor("bcolv", [128, 16], F32))

            for c16 in range(16):
                sem_g = ["LX", "LD", "LW1_0", "LW1_1"][c16 % 4]

                def _ldy(sync, c16=c16, sem_g=sem_g):
                    if c16 == 0:
                        S.wait(sync, "SC", SC_ALL)
                    if c16 >= 4:
                        S.wait(sync, "AC", ck[("L2s2", c16 - 4)])
                        S.wait(sync, "DV", ck[("L2s1", c16 - 4)])
                    mi = sync.dma_start(out=ysb[:, c16, :],
                                        in_=ybuf[c16 * 128:(c16 + 1) * 128, :])
                    ck[("L2ld", c16)] = S.inc(mi, sem_g, 16)
                block.sync(_ldy)

                def _dvl1(v, c16=c16, sem_g=sem_g):
                    S.wait(v, sem_g, ck[("L2ld", c16)])
                    mi = v.tensor_reduce(mcolv[:, c16:c16 + 1], ysb[:, c16, :],
                                         AX.X, ALU.add)
                    ck[("L2s1", c16)] = S.inc(mi, "DV")
                block.vector(_dvl1)

                def _acl(a, c16=c16, sem_g=sem_g):
                    S.wait(a, sem_g, ck[("L2ld", c16)])
                    if c16 >= 2:
                        S.wait(a, "AC", ck[("L2s2", c16 - 2)])
                    mi = a.activation(sqs[:, c16 % 2, :], ysb[:, c16, :],
                                      AF.Square,
                                      accum_out=scolv[:, c16:c16 + 1])
                    ck[("L2s2", c16)] = S.inc(mi, "AC")
                block.scalar(_acl)

            def _dvb(v):
                S.wait(v, "AC", ck[("L2s2", 15)])
                S.wait(v, "DV", ck[("L2s1", 15)])
                c = S.inc(v.tensor_scalar(mcolv[:], mcolv[:], 1.0 / C, None,
                                          ALU.mult), "DV")
                S.wait(v, "DV", c)
                c = S.inc(v.tensor_scalar(scolv[:], scolv[:], 1.0 / C, None,
                                          ALU.mult), "DV")
                S.wait(v, "DV", c)
                c = S.inc(v.tensor_tensor(vcolv[:], mcolv[:], mcolv[:], ALU.mult),
                          "DV")
                S.wait(v, "DV", c)
                c = S.inc(v.tensor_tensor(vcolv[:], scolv[:], vcolv[:],
                                          ALU.subtract), "DV")
                S.wait(v, "DV", c)
                c = S.inc(v.tensor_scalar(vcolv[:], vcolv[:], EPS_LN, None,
                                          ALU.add), "DV")
                ck["L2v"] = c
            block.vector(_dvb)

            def _acb(a):
                S.wait(a, "DV", ck["L2v"])
                mi = a.activation(sdcv[:], vcolv[:], AF.Sqrt)
                ck["L2sd"] = S.inc(mi, "AC")
            block.scalar(_acb)

            def _dvb2(v):
                S.wait(v, "AC", ck["L2sd"])
                c = S.inc(v.reciprocal(rscv[:], sdcv[:]), "DV")
                S.wait(v, "DV", c)
                c = S.inc(v.tensor_tensor(bcolv[:], mcolv[:], rscv[:], ALU.mult),
                          "DV")
                S.wait(v, "DV", c)
                mi = v.tensor_scalar(bcolv[:], bcolv[:], -1.0, None, ALU.mult)
                ck["L2r"] = S.inc(mi, "DV")
            block.vector(_dvb2)

            for c16 in range(16):
                def _aca(a, c16=c16):
                    S.wait(a, "DV", ck["L2r"])
                    if c16 >= 2:
                        S.wait(a, f"ST{c16 % 2}", ck[("L2st", c16 - 2)])
                    mi = a.activation(sqs[:, c16 % 2, :], ysb[:, c16, :],
                                      AF.Identity,
                                      bias=bcolv[:, c16:c16 + 1],
                                      scale=rscv[:, c16:c16 + 1])
                    ck[("L2ap", c16)] = S.inc(mi, "AC")
                block.scalar(_aca)

                def _sto(sync, c16=c16):
                    S.wait(sync, "AC", ck[("L2ap", c16)])
                    mi = sync.dma_start(out=out[c16 * 128:(c16 + 1) * 128, :],
                                        in_=sqs[:, c16 % 2, :])
                    ck[("L2st", c16)] = S.inc(mi, f"ST{c16 % 2}", 16)
                block.sync(_sto)

            def _fin(sync):
                sync.wait_ge(S.h["ST0"], S.val("ST0"))
                sync.wait_ge(S.h["ST1"], S.val("ST1"))
            block.sync(_fin)

    return nc


def _emit_ln(nc, block, S, ck, tag, y_ap, sq_ap, dst_ap, dep_dv, dep_pe_sq,
             onesc, ones1, R0, R1, R2, R3, R4, sq_rotate, mid=None):
    phase, tile = tag
    with contextlib.ExitStack() as ps:
        ps1 = ps.enter_context(nc.psum_tensor(f"ps1{phase}{tile}", [1, NT], F32))
        ps2 = ps.enter_context(nc.psum_tensor(f"ps2{phase}{tile}", [1, NT], F32))
        pA = ps.enter_context(nc.psum_tensor(f"pA{phase}{tile}", [128, NT], F32))
        pB = ps.enter_context(nc.psum_tensor(f"pB{phase}{tile}", [128, NT], F32))

        for oc in range(CC):
            def _acs(a, oc=oc):
                S.wait(a, "DV", dep_dv(oc))
                if oc == 0:
                    S.wait(a, "PE", dep_pe_sq)
                if sq_rotate and oc >= 2:
                    S.wait(a, "PE", ck[("LN", phase, tile, "s2", oc - 2)])
                mi = a.activation(sq_ap(oc), y_ap(oc), AF.Square)
                ck[("LN", phase, tile, "sq", oc)] = S.inc(mi, "AC")
            block.scalar(_acs)

            def _pes(t, oc=oc):
                if oc == 0:
                    S.wait(t, "DV", dep_dv(CC - 1))
                t.matmul(ps1[:], onesc[:], y_ap(oc), start=(oc == 0),
                         stop=(oc == CC - 1))
                S.wait(t, "AC", ck[("LN", phase, tile, "sq", oc)])
                mi = t.matmul(ps2[:], onesc[:], sq_ap(oc), start=(oc == 0),
                              stop=(oc == CC - 1))
                ck[("LN", phase, tile, "s2", oc)] = S.inc(mi, "PE")
            block.tensor(_pes)

        if mid is not None:
            mid()

        def _dvs(v):
            S.wait(v, "PE", ck[("LN", phase, tile, "s2", CC - 1)])
            v.tensor_scalar(R0[:], ps1[:], 1.0 / C, None, ALU.mult)
            c = S.inc(v.tensor_scalar(R1[:], ps2[:], 1.0 / C, None, ALU.mult), "DV")
            S.wait(v, "DV", c)
            c = S.inc(v.tensor_tensor(R2[:], R0[:], R0[:], ALU.mult), "DV")
            S.wait(v, "DV", c)
            c = S.inc(v.tensor_tensor(R1[:], R1[:], R2[:], ALU.subtract), "DV")
            S.wait(v, "DV", c)
            mi = v.tensor_scalar(R1[:], R1[:], EPS_LN, None, ALU.add)
            ck[("LN", phase, tile, "var")] = S.inc(mi, "DV")
        block.vector(_dvs)

        def _acr(a):
            S.wait(a, "DV", ck[("LN", phase, tile, "var")])
            mi = a.activation(R3[:], R1[:], AF.Sqrt)
            ck[("LN", phase, tile, "std")] = S.inc(mi, "AC")
        block.scalar(_acr)

        def _dva(v):
            S.wait(v, "AC", ck[("LN", phase, tile, "std")])
            c = S.inc(v.reciprocal(R4[:], R3[:]), "DV")
            S.wait(v, "DV", c)
            mi = v.scalar_tensor_tensor(R2[:], R0[:], -1.0, R4[:], ALU.mult, ALU.mult)
            ck[("LN", phase, tile, "AB")] = S.inc(mi, "DV")
        block.vector(_dva)

        def _peb(t):
            S.wait(t, "DV", ck[("LN", phase, tile, "AB")])
            t.matmul(pA[:], ones1[:], R4[:], start=True, stop=True)
            mi = t.matmul(pB[:], ones1[:], R2[:], start=True, stop=True)
            ck[("LN", phase, tile, "bc")] = S.inc(mi, "PE")
        block.tensor(_peb)

        def _dvap(v):
            S.wait(v, "PE", ck[("LN", phase, tile, "bc")])
            for oc in range(CC):
                mi = v.tensor_tensor(y_ap(oc), y_ap(oc), pA[:], ALU.mult)
            c = S.inc(mi, "DV")
            S.wait(v, "DV", c)
            for oc in range(CC):
                mi = v.tensor_tensor(dst_ap(oc), y_ap(oc), pB[:], ALU.add)
            ck[("LN", phase, tile, "app2")] = S.inc(mi, "DV")
        block.vector(_dvap)




_NC_CACHE = {}


def _get_nc():
    if "nc" not in _NC_CACHE:
        _NC_CACHE["nc"] = build_nc()
    return _NC_CACHE["nc"]


def _prep_weights(Wqkv, Wproj, bproj, sW1, sb1, sW2, sb2, rW1, rb1, rW2, rb2,
                  gW, gb):
    f = np.float32
    d = {}
    d["wqkvT"] = np.ascontiguousarray(np.asarray(Wqkv, f).T)
    d["wprojT"] = np.ascontiguousarray(np.asarray(Wproj, f).T)
    d["bproj"] = np.asarray(bproj, f).reshape(1, C)
    d["gw"] = np.ascontiguousarray(np.asarray(gW, f))
    d["gb"] = np.asarray(gb, f).reshape(1, ER)
    sW1 = np.asarray(sW1, f)[0]
    d["sw1"] = np.ascontiguousarray(
        sW1.reshape(CC, 128, NFC_S, 128).transpose(2, 1, 0, 3))
    d["sw2"] = np.ascontiguousarray(np.asarray(sW2, f)[0].reshape(NFC_S, 128, C))
    rW1 = np.asarray(rW1, f)
    d["rw1"] = np.ascontiguousarray(
        rW1.reshape(ER, CC, 128, NFC, 128).transpose(0, 3, 2, 1, 4))
    d["rw2"] = np.ascontiguousarray(np.asarray(rW2, f).reshape(ER, NFC, 128, C))
    d["b1r"] = np.ascontiguousarray(
        np.asarray(rb1, f).reshape(ER, NFC, 128).transpose(2, 0, 1))
    d["b1s"] = np.ascontiguousarray(np.asarray(sb1, f)[0].reshape(NFC_S, 128).T)
    d["b2cat"] = np.ascontiguousarray(
        np.concatenate([np.asarray(rb2, f), np.asarray(sb2, f)], axis=0))
    d["ident"] = np.eye(128, dtype=f)
    d["ltri"] = np.triu(np.ones((128, 128), f), 1)
    d["tokid"] = np.ascontiguousarray(
        (np.arange(16)[None, :] * 128 + np.arange(128)[:, None]).astype(np.uint32))
    d["idinit"] = np.full((1, CAP), int(BIGF), np.uint32)
    bc = np.zeros((16, HD, 128), f)
    for ec in range(HD):
        bc[2 * ec, ec, 0:64] = 1.0
        bc[2 * ec + 1, ec, 64:128] = 1.0
    d["bc16"] = bc
    return d


def make_in_maps(inputs):
    x = np.asarray(inputs["L0_attn_x"], np.float32)
    w = _prep_weights(inputs["L0_attn_Wqkv"], inputs["L0_attn_Wproj"],
                      inputs["L0_attn_bproj"], inputs["L0_ffn_sW1"],
                      inputs["L0_ffn_sb1"], inputs["L0_ffn_sW2"],
                      inputs["L0_ffn_sb2"], inputs["L0_ffn_rW1"],
                      inputs["L0_ffn_rb1"], inputs["L0_ffn_rW2"],
                      inputs["L0_ffn_rb2"], inputs["L0_ffn_gW"],
                      inputs["L0_ffn_gb"])
    in_maps = []
    for core in range(8):
        seq = core // 4
        o0 = (core % 4) * TOWN
        m = dict(w)
        m["xT"] = np.ascontiguousarray(x[seq][o0:o0 + TOWN].T)
        in_maps.append(m)
    return in_maps


def kernel(**inputs):
    in_maps = make_in_maps(inputs)
    nc = _get_nc()
    res = run_bass_kernel_spmd(nc, in_maps, core_ids=list(range(8)))
    full = np.empty((2, TSEQ, C), np.float32)
    for core in range(8):
        seq = core // 4
        o0 = (core % 4) * TOWN
        full[seq, o0:o0 + TOWN] = res.results[core]["out"]
    return full
